# version 32
# baseline (speedup 1.0000x reference)
import numpy as np
import concourse.bass as bass
import concourse.mybir as mybir
from concourse.bass_utils import run_bass_kernel_spmd
from contextlib import ExitStack

F32 = mybir.dt.float32
BF16 = mybir.dt.bfloat16
F16 = mybir.dt.float16
FP8 = mybir.dt.float8e4
AF = mybir.ActivationFunctionType
ALU = mybir.AluOpType
AXX = mybir.AxisListType.X

NCORES = 8
BATCH = 2
D = 1024
H = 4096
RANK = 64
NST = 16
DT = D // 128
HT = H // 128
CH = 256
EPS = 1e-5


def build(lcore=1024):
    ntok = BATCH * lcore
    nt = ntok // 128
    ncpb = lcore // CH
    nch = BATCH * ncpb
    tpc = CH // 128
    nhalf = 2
    htok = ntok // nhalf
    hnt = nt // nhalf

    nc = bass.Bass("TRN2", target_bir_lowering=False, debug=False)

    x_in = nc.dram_tensor("x_in", [ntok, D], F32, kind="ExternalInput").ap()
    w1t = nc.dram_tensor("w1t", [HT, 128, DT, 128], F32, kind="ExternalInput").ap()
    w2t = nc.dram_tensor("w2t", [HT, 128, D], F32, kind="ExternalInput").ap()
    b1t = nc.dram_tensor("b1t", [128, HT], F32, kind="ExternalInput").ap()
    b2bc = nc.dram_tensor("b2bc", [128, D], F32, kind="ExternalInput").ap()
    tebc = nc.dram_tensor("tebc", [128, D], F32, kind="ExternalInput").ap()
    dbc = nc.dram_tensor("dbc", [128, D], F32, kind="ExternalInput").ap()
    alogT = nc.dram_tensor("alogT", [NST, D], F32, kind="ExternalInput").ap()
    xpwd = nc.dram_tensor("xpwd", [128, DT, RANK], F32, kind="ExternalInput").ap()
    xpwb = nc.dram_tensor("xpwb", [128, DT, NST], F32, kind="ExternalInput").ap()
    xpwc = nc.dram_tensor("xpwc", [128, DT, NST], F32, kind="ExternalInput").ap()
    dtwtb = nc.dram_tensor("dtwtb", [RANK + 1, D], F32, kind="ExternalInput").ap()
    identf = nc.dram_tensor("identf", [128, 128], F32, kind="ExternalInput").ap()
    utf = nc.dram_tensor("utf", [128, 128], F32, kind="ExternalInput").ap()
    maskin = nc.dram_tensor("maskin", [128, NCORES], F32, kind="ExternalInput").ap()
    tet = nc.dram_tensor("tet", [128, DT], F32, kind="ExternalInput").ap()
    y_out = nc.dram_tensor("y_out", [ntok, D], F32, kind="ExternalOutput").ap()

    cc_in = nc.dram_tensor("cc_in", [128, 2 * BATCH * 128], F32)
    cc_out = nc.dram_tensor("cc_out", [128 * NCORES, 2 * BATCH * 128], F32, addr_space="Shared")

    es = ExitStack()
    ec = es.enter_context

    sx = ec(nc.sbuf_tensor("sx", [128, nt, D], F32))
    s_idb = ec(nc.sbuf_tensor("s_idb", [128, 128], BF16))
    s_utb = ec(nc.sbuf_tensor("s_utb", [128, 128], BF16))
    s_onb = ec(nc.sbuf_tensor("s_onb", [128, 128], BF16))
    s_b1 = ec(nc.sbuf_tensor("s_b1", [128, HT], F32))
    s_xpwd = ec(nc.sbuf_tensor("s_xpwd", [128, DT, RANK], BF16))
    s_xpwb = ec(nc.sbuf_tensor("s_xpwb", [128, DT, NST], BF16))
    s_xpwc = ec(nc.sbuf_tensor("s_xpwc", [128, DT, NST], BF16))
    s_mask = ec(nc.sbuf_tensor("s_mask", [128, NCORES], F32))
    s_sigc = ec(nc.sbuf_tensor("s_sigc", [128, DT], F32))
    s_bn = ec(nc.sbuf_tensor("s_bn", [128, 12], F32))
    s_mv = ec(nc.sbuf_tensor("s_mv", [128, 32], F32))
    s_rstd = ec(nc.sbuf_tensor("s_rstd", [128, 16], F32))
    s_mrs = ec(nc.sbuf_tensor("s_mrs", [128, 16], F32))

    names = ("LD", "LA", "LB", "LC", "GW", "W1a", "W1b", "RSL", "RSQ", "RSH", "RSH2", "RSH3", "RSH4", "XA", "XB", "XC", "XD", "PE", "SC", "DV", "GP", "CC", "ST")
    SEM = {n: ec(nc.semaphore("s" + n)) for n in names}
    C = {n: 0 for n in names}

    def inc(instr, name, n=1):
        C[name] += n
        instr.then_inc(SEM[name], n)

    def wt(eng, name):
        if C[name] > 0:
            eng.wait_ge(SEM[name], C[name])

    def pw(eng, dep):
        if dep[0] is not None and dep[1] > 0:
            eng.wait_ge(SEM[dep[0]], dep[1])

    block = ec(nc.Block())

    xldm = {}

    @block.sync
    def _(sync):
        inc(sync.dma_start(out=s_b1[:], in_=b1t), "LD", 16)
        inc(sync.dma_start(out=s_mask[:], in_=maskin), "LD", 16)
        inc(sync.dma_start(out=s_sigc[:], in_=tet), "LD", 16)

    @block.gpsimd
    def _(gp):
        inc(gp.dma_start(out=s_idb[:], in_=identf), "GW", 16)
        inc(gp.dma_start(out=s_utb[:], in_=utf), "GW", 16)
        inc(gp.dma_start(out=s_xpwd[:], in_=xpwd), "GW", 16)
        inc(gp.dma_start(out=s_xpwb[:], in_=xpwb), "GW", 16)
        inc(gp.dma_start(out=s_xpwc[:], in_=xpwc), "GW", 16)

    xsem = {0: "XA", 2: "LB", 1: "LA", 3: "XB"}
    for t in range(4):
        nm = xsem.get(t, "XD" if t % 2 == 0 else "XC")
        if t % 2 == 0:
            @block.sync
            def _(sync, t=t, nm=nm):
                inc(sync.dma_start(out=sx[:, t, :], in_=x_in[t * 128:(t + 1) * 128, :]), nm, 16)
        else:
            @block.gpsimd
            def _(gp, t=t, nm=nm):
                inc(gp.dma_start(out=sx[:, t, :], in_=x_in[t * 128:(t + 1) * 128, :]), nm, 16)
        xldm[t] = (nm, C[nm])

    @block.vector
    def _(ve):
        wt(ve, "LD")
        ve.memset(s_onb[:], 1.0)
        inc(ve.drain(), "DV")
    dv_cst = C["DV"]

    @block.scalar
    def _(sc, dv_cst=dv_cst):
        sc.wait_ge(SEM["DV"], dv_cst)
        sc.activation(s_sigc[:], s_sigc[:], AF.Sigmoid)
        inc(sc.drain(), "SC")
    sc_sigc = C["SC"]

    @block.vector
    def _(ve, sc_sigc=sc_sigc):
        ve.wait_ge(SEM["SC"], sc_sigc)
        wt(ve, "GW")
        for g in range(DT):
            ve.tensor_scalar_mul(s_xpwd[:, g, :], s_xpwd[:, g, :], s_sigc[:, g:g + 1])
            ve.drain()
            ve.tensor_scalar_mul(s_xpwb[:, g, :], s_xpwb[:, g, :], s_sigc[:, g:g + 1])
            ve.drain()
            ve.tensor_scalar_mul(s_xpwc[:, g, :], s_xpwc[:, g, :], s_sigc[:, g:g + 1])
            ve.drain()
        inc(ve.drain(), "DV")

    def emit_ln_range(tiles, out_ap_fn, first_wait=None, norm_wait=0, idx0=0):
        ntl = len(tiles)
        i0 = idx0
        @block.vector
        def _(ve):
            if first_wait is None:
                wt(ve, "SC")
            else:
                pw(ve, first_wait)
            for k, t in enumerate(tiles):
                ve.bn_stats(s_bn[:, 0:6], sx[:, t, 0:512])
                ve.bn_stats(s_bn[:, 6:12], sx[:, t, 512:1024])
                ve.drain()
                ve.bn_aggr(s_mv[:, 2 * (i0 + k):2 * (i0 + k) + 2], s_bn[:])
                ve.drain()
            for k in range(ntl):
                ve.tensor_scalar_add(s_rstd[:, i0 + k:i0 + k + 1], s_mv[:, 2 * (i0 + k) + 1:2 * (i0 + k) + 2], EPS)
                ve.drain()
            inc(ve.drain(), "DV")

        @block.scalar
        def _(sc):
            wt(sc, "DV")
            sc.activation(s_rstd[:, i0:i0 + ntl], s_rstd[:, i0:i0 + ntl], AF.Sqrt)
            inc(sc.drain(), "SC")

        @block.vector
        def _(ve):
            wt(ve, "SC")
            ve.reciprocal(s_rstd[:, i0:i0 + ntl], s_rstd[:, i0:i0 + ntl])
            ve.drain()
            for k in range(ntl):
                ve.tensor_mul(s_mrs[:, i0 + k:i0 + k + 1], s_mv[:, 2 * (i0 + k):2 * (i0 + k) + 1], s_rstd[:, i0 + k:i0 + k + 1])
                ve.drain()
            ve.tensor_scalar_mul(s_mrs[:, i0:i0 + ntl], s_mrs[:, i0:i0 + ntl], -1.0)
            inc(ve.drain(), "DV")

        @block.scalar
        def _(sc, norm_wait=norm_wait):
            wt(sc, "DV")
            if norm_wait > 0:
                sc.wait_ge(SEM["PE"], norm_wait)
            for k, t in enumerate(tiles):
                sc.activation(out_ap_fn(k, t), sx[:, t, :], AF.Identity,
                              bias=s_mrs[:, i0 + k:i0 + k + 1], scale=s_rstd[:, i0 + k:i0 + k + 1])
            inc(sc.drain(), "SC")
        return C["SC"]

    esM = ExitStack()
    s_abc = esM.enter_context(nc.sbuf_tensor("s_abc", [128, D], BF16))
    sE = esM.enter_context(nc.sbuf_tensor("sE", [128, nt, D], BF16))
    s_dp = esM.enter_context(nc.sbuf_tensor("s_dp", [RANK + 1, ntok], BF16))
    sC = esM.enter_context(nc.sbuf_tensor("sC", [NST, ntok], BF16))
    s_qpk = esM.enter_context(nc.sbuf_tensor("s_qpk", [128, nch, 128], F32))
    s_btm = esM.enter_context(nc.sbuf_tensor("s_btm", [128, nt, NST], BF16))
    s_gm = esM.enter_context(nc.sbuf_tensor("s_gm", [128, nch, 3 * 128], BF16))
    s_lppk = esM.enter_context(nc.sbuf_tensor("s_lppk", [128, nch, 128], F32))
    s_dtw = esM.enter_context(nc.sbuf_tensor("s_dtw", [RANK + 1, D], BF16))
    esU = ExitStack()
    s_u = esU.enter_context(nc.sbuf_tensor("s_u", [128, nt, D], BF16))

    esA = ExitStack()
    s_alT = esA.enter_context(nc.sbuf_tensor("s_alT", [NST, D], F32))
    s_on16 = esA.enter_context(nc.sbuf_tensor("s_on16", [NST, 128], F32))
    p_bc = esA.enter_context(nc.psum_tensor("p_bc", [128, D], F32))


    @block.sync
    def _(sync):
        inc(sync.dma_start(out=s_alT[:], in_=alogT), "LC", 16)

    @block.gpsimd
    def _(gp):
        inc(gp.dma_start(out=s_dtw[:], in_=dtwtb), "XC", 16)

    for t in range(4, nt):
        nm = xsem.get(t, "XD" if t % 2 == 0 else "XC")
        if t % 2 == 0:
            @block.sync
            def _(sync, t=t, nm=nm):
                inc(sync.dma_start(out=sx[:, t, :], in_=x_in[t * 128:(t + 1) * 128, :]), nm, 16)
        else:
            @block.gpsimd
            def _(gp, t=t, nm=nm):
                inc(gp.dma_start(out=sx[:, t, :], in_=x_in[t * 128:(t + 1) * 128, :]), nm, 16)
        xldm[t] = (nm, C[nm])

    @block.vector
    def _(ve):
        wt(ve, "LC")
        ve.memset(s_on16[:], 1.0)
        inc(ve.drain(), "DV")

    @block.tensor
    def _(te):
        wt(te, "DV")
        te.matmul(p_bc[:, 0:512], s_on16[:], s_alT[:, 0:512], start=True, stop=True)
        inc(te.matmul(p_bc[:, 512:1024], s_on16[:], s_alT[:, 512:1024],
                      start=True, stop=True), "PE")

    @block.scalar
    def _(sc):
        wt(sc, "PE")
        sc.activation(s_abc[:], p_bc[:], AF.Exp, scale=1.0 / NST)
        inc(sc.drain(), "SC")

    @block.vector
    def _(ve):
        wt(ve, "SC")
        ve.tensor_scalar_mul(s_abc[:], s_abc[:], -1.0)
        inc(ve.drain(), "DV")
    esA.close()

    esPA = ExitStack()
    esDB = ExitStack()
    s_dbc = esDB.enter_context(nc.sbuf_tensor("s_dbc", [128, D], BF16))
    s_sig = esDB.enter_context(nc.sbuf_tensor("s_sig", [128, D], BF16))
    s_sigD = esDB.enter_context(nc.sbuf_tensor("s_sigD", [128, D], BF16))
    s_gpscr = esDB.enter_context(nc.sbuf_tensor("s_gpscr", [128, D], F32))
    esB = ExitStack()
    sB = esB.enter_context(nc.sbuf_tensor("sB", [NST, ntok], BF16))
    s_ut = esPA.enter_context(nc.sbuf_tensor("s_ut", [128, DT, 512], BF16))
    p_t = esPA.enter_context(nc.psum_tensor("p_t", [128, 512], BF16))
    p_t2 = esPA.enter_context(nc.psum_tensor("p_t2", [128, 512], BF16))
    p_d = esPA.enter_context(nc.psum_tensor("p_d", [RANK, 512], F32))
    p_b = esPA.enter_context(nc.psum_tensor("p_b", [NST, 512], F32))
    p_c = esPA.enter_context(nc.psum_tensor("p_c", [NST, 512], F32))
    p_g = esPA.enter_context(nc.psum_tensor("p_g", [128, 512], F32))

    @block.gpsimd
    def _(gp):
        wt(gp, "SC")
        wt(gp, "DV")
        inc(gp.dma_start(out=s_sig[:], in_=tebc), "GW", 16)
        inc(gp.dma_start(out=s_dbc[:], in_=dbc), "GW", 16)
    gw_sig = C["GW"]

    @block.scalar
    def _(sc, gw_sig=gw_sig):
        sc.wait_ge(SEM["GW"], gw_sig)
        sc.activation(s_sig[:], s_sig[:], AF.Sigmoid)
        inc(sc.drain(), "SC")
    sc_sig = C["SC"]

    @block.vector
    def _(ve):
        wt(ve, "SC")
        ve.memset(s_dp[RANK:RANK + 1, :], 1.0)
        inc(ve.drain(), "DV")

    ngrp = nt // 4
    pt_free = [(None, 0), (None, 0)]
    pdbc_free = (None, 0)
    for g4 in range(ngrp):
        tiles = [g4 * 4 + q for q in range(4)]

        @block.vector
        def _(ve, tiles=tiles, g4=g4):
            if g4 == 0:
                wt(ve, "SC")
                for t in tiles:
                    pw(ve, xldm[t])
            else:
                wt(ve, "XD")
                wt(ve, "XC")
            for t in tiles:
                ve.bn_stats(s_bn[:, 0:6], sx[:, t, 0:512])
                ve.bn_stats(s_bn[:, 6:12], sx[:, t, 512:1024])
                ve.drain()
                ve.bn_aggr(s_mv[:, 2 * t:2 * t + 2], s_bn[:])
                ve.drain()
                ve.tensor_scalar_add(s_rstd[:, t:t + 1], s_mv[:, 2 * t + 1:2 * t + 2], EPS)
                ve.drain()
            inc(ve.drain(), "DV")
        dv_st = C["DV"]

        @block.scalar
        def _(sc, tiles=tiles, dv_st=dv_st):
            sc.wait_ge(SEM["DV"], dv_st)
            sc.activation(s_rstd[:, tiles[0]:tiles[0] + 4], s_rstd[:, tiles[0]:tiles[0] + 4], AF.Sqrt)
            inc(sc.drain(), "SC")
        sc_sq = C["SC"]

        @block.vector
        def _(ve, tiles=tiles, sc_sq=sc_sq):
            ve.wait_ge(SEM["SC"], sc_sq)
            t0g = tiles[0]
            ve.reciprocal(s_rstd[:, t0g:t0g + 4], s_rstd[:, t0g:t0g + 4])
            ve.drain()
            for t in tiles:
                ve.tensor_mul(s_mrs[:, t:t + 1], s_mv[:, 2 * t:2 * t + 1], s_rstd[:, t:t + 1])
                ve.drain()
            ve.tensor_scalar_mul(s_mrs[:, t0g:t0g + 4], s_mrs[:, t0g:t0g + 4], -1.0)
            inc(ve.drain(), "DV")
        dv_ms = C["DV"]

        @block.scalar
        def _(sc, tiles=tiles, dv_ms=dv_ms):
            sc.wait_ge(SEM["DV"], dv_ms)
            for t in tiles:
                sc.activation(s_u[:, t, :], sx[:, t, :], AF.Identity,
                              bias=s_mrs[:, t:t + 1], scale=s_rstd[:, t:t + 1])
            inc(sc.drain(), "SC")
        sc_no = C["SC"]

        gp_us = None

        for g in range(DT):
            par = g % 2
            ptr = p_t if par == 0 else p_t2

            @block.tensor
            def _(te, tiles=tiles, g=g, sc_no=sc_no, ptf=pt_free[g % 2], ptr=ptr):
                te.wait_ge(SEM["SC"], sc_no)
                pw(te, ptf)
                for q in range(4):
                    inc(te.transpose(ptr[:, q * 128:(q + 1) * 128],
                                     s_u[:, tiles[q], g * 128:(g + 1) * 128],
                                     s_idb[:]), "PE")
            pe_t = C["PE"]

            @block.vector
            def _(ve, g=g, pe_t=pe_t, ptr=ptr):
                ve.wait_ge(SEM["PE"], pe_t)
                ve.tensor_copy(s_ut[:, g, :], ptr[:])
                inc(ve.drain(), "DV")
            pt_free[par] = ("DV", C["DV"])
        dv_ut = C["DV"]

        @block.tensor
        def _(te, g4=g4, dv_ut=dv_ut, pdbcf=pdbc_free):
            te.wait_ge(SEM["DV"], dv_ut)
            pw(te, pdbcf)
            for g in range(DT):
                te.ldweights(s_xpwd[:, g, :])
                te.matmul(p_d[:], s_xpwd[:, g, :], s_ut[:, g, :],
                          start=(g == 0), stop=(g == DT - 1))
            for g in range(DT):
                te.ldweights(s_xpwb[:, g, :])
                te.matmul(p_b[:], s_xpwb[:, g, :], s_ut[:, g, :],
                          start=(g == 0), stop=(g == DT - 1))
            for g in range(DT):
                te.ldweights(s_xpwc[:, g, :])
                mm = te.matmul(p_c[:], s_xpwc[:, g, :], s_ut[:, g, :],
                               start=(g == 0), stop=(g == DT - 1))
            inc(mm, "PE")
        pe_xz = C["PE"]

        @block.scalar
        def _(sc, g4=g4, pe_xz=pe_xz):
            sc.wait_ge(SEM["PE"], pe_xz)
            sc.activation(s_dp[0:RANK, g4 * 512:(g4 + 1) * 512], p_d[:], AF.Identity)
            sc.activation(sB[:, g4 * 512:(g4 + 1) * 512], p_b[:], AF.Identity)
            sc.activation(sC[:, g4 * 512:(g4 + 1) * 512], p_c[:], AF.Identity)
            inc(sc.drain(), "SC")
        pdbc_free = ("SC", C["SC"])

    @block.tensor
    def _(te):
        wt(te, "SC")
        pw(te, pt_free[0])
        for tt in range(nt):
            inc(te.transpose(p_t[:, (tt % 32) * NST:((tt % 32) + 1) * NST],
                             sB[:, tt * 128:(tt + 1) * 128],
                             s_idb[0:NST, 0:NST]), "PE")

    @block.vector
    def _(ve):
        wt(ve, "PE")
        ve.tensor_copy(s_btm[:], p_t[:, 0:nt * NST])
        inc(ve.drain(), "DV")

    gmf = (None, 0)
    for c in range(nch):
        t0 = c * tpc

        @block.tensor
        def _(te, c=c, t0=t0, gmf=gmf):
            if c == 0:
                wt(te, "DV")
            pw(te, gmf)
            inc(te.matmul(p_g[:, 0:128], sB[:, t0 * 128:(t0 + 1) * 128],
                          sC[:, t0 * 128:(t0 + 1) * 128], start=True, stop=True), "PE")
            inc(te.matmul(p_g[:, 128:256], sB[:, t0 * 128:(t0 + 1) * 128],
                          sC[:, (t0 + 1) * 128:(t0 + 2) * 128], start=True, stop=True), "PE")
            inc(te.matmul(p_g[:, 256:384], sB[:, (t0 + 1) * 128:(t0 + 2) * 128],
                          sC[:, (t0 + 1) * 128:(t0 + 2) * 128], start=True, stop=True), "PE")
        pe_gm = C["PE"]

        @block.vector
        def _(ve, c=c, pe_gm=pe_gm):
            ve.wait_ge(SEM["PE"], pe_gm)
            ve.tensor_mul(s_gm[:, c, 0:128], p_g[:, 0:128], s_utb[:])
            ve.tensor_copy(s_gm[:, c, 128:256], p_g[:, 128:256])
            ve.tensor_mul(s_gm[:, c, 256:384], p_g[:, 256:384], s_utb[:])
            inc(ve.drain(), "DV")
        gmf = ("DV", C["DV"])
    esPA.close()
    esB.close()

    esP1 = ExitStack()
    s_delta = esP1.enter_context(nc.sbuf_tensor("s_delta", [128, tpc, D], F32))
    s_we = esP1.enter_context(nc.sbuf_tensor("s_we", [128, tpc, 512], BF16))
    s_da = esP1.enter_context(nc.sbuf_tensor("s_da", [128, 2, tpc, 512], BF16))
    s_v0 = esP1.enter_context(nc.sbuf_tensor("s_v0", [128, tpc, 512], BF16))
    s_wc = esP1.enter_context(nc.sbuf_tensor("s_wc", [128, tpc, 512], BF16))
    s_y = esP1.enter_context(nc.sbuf_tensor("s_y", [128, 2, tpc, 512], BF16))
    s_n16l = esP1.enter_context(nc.sbuf_tensor("s_n16l", [NST, 2, 512], F32))
    s_n16q = esP1.enter_context(nc.sbuf_tensor("s_n16q", [NST, 2, 512], F32))
    s_utr = esP1.enter_context(nc.sbuf_tensor("s_utr", [128, tpc, D], BF16))
    s_dum = esP1.enter_context(nc.sbuf_tensor("s_dum", [1, 1], F32))
    p_z0 = esP1.enter_context(nc.psum_tensor("p_z0", [128, 512], F32))
    p_z1 = esP1.enter_context(nc.psum_tensor("p_z1", [128, 512], F32))
    p_c0 = esP1.enter_context(nc.psum_tensor("p_c0", [128, 512], F32))
    p_c1 = esP1.enter_context(nc.psum_tensor("p_c1", [128, 512], F32))
    p_y0 = esP1.enter_context(nc.psum_tensor("p_y0", [128, 512], F32))
    p_y1 = esP1.enter_context(nc.psum_tensor("p_y1", [128, 512], F32))
    p_lp = esP1.enter_context(nc.psum_tensor("p_lp", [NST, 512], F32))
    p_q = esP1.enter_context(nc.psum_tensor("p_q", [NST, 512], F32))

    sc_done = C["SC"]
    dv_done = C["DV"]

    @block.gpsimd
    def _(gp):
        gp.wait_ge(SEM["SC"], sc_done)
        wt(gp, "GW")
        wt(gp, "LD")
        gp.tensor_mul(s_sigD[:], s_sig[:], s_dbc[:])
        inc(gp.drain(), "GP")

    NU = nch * 2
    utrm = {}
    dzm = {}
    ksp = {}
    k2w = {}
    k2e = {}
    dam = {}
    v0m = {}
    cdm = {}
    p16m = {}
    lpcm = {}
    wm = {}
    qm = {}
    qevm = {}
    ym = {}
    ymulm = {}
    yaddm = {}
    rslc = {}
    rsqc = {}

    def HSL(h):
        return slice(h * 512, (h + 1) * 512)

    def emit_dz(u):
        c, h = u // 2, u % 2
        tt0, tt1 = c * tpc, c * tpc + 1

        @block.tensor
        def _(te, u=u, tt0=tt0, tt1=tt1, h=h):
            if u == 0:
                te.wait_ge(SEM["SC"], sc_done)
                te.wait_ge(SEM["DV"], dv_done)
            else:
                pw(te, ("SC", ksp[u - 1]))
            te.ldweights(s_dp[:, tt0 * 128:(tt0 + 1) * 128])
            te.matmul(p_z0[:], s_dp[:, tt0 * 128:(tt0 + 1) * 128],
                      s_dtw[:, HSL(h)], start=True, stop=True)
            te.ldweights(s_dp[:, tt1 * 128:(tt1 + 1) * 128])
            inc(te.matmul(p_z1[:], s_dp[:, tt1 * 128:(tt1 + 1) * 128],
                          s_dtw[:, HSL(h)], start=True, stop=True), "PE")
        dzm[u] = C["PE"]

    def emit_ew_final(u):
        c, h = u // 2, u % 2
        tt0, tt1 = c * tpc, c * tpc + 1

        @block.scalar
        def _(sc, u=u, h=h, tt0=tt0, tt1=tt1):
            pw(sc, ("PE", cdm[u]))
            pw(sc, ("DV", wm[u - 1]))
            sc.activation(s_we[:, 0, :], p_c0[:], AF.Exp, scale=-1.0)
            sc.drain()
            sc.activation(s_we[:, 1, :], p_c1[:], AF.Exp, scale=-1.0)
            inc(sc.drain(), "SC")
            sc.activation(sE[:, tt0, HSL(h)], p_c0[:], AF.Exp)
            sc.drain()
            sc.activation(sE[:, tt1, HSL(h)], p_c1[:], AF.Exp)
            inc(sc.drain(), "SC")
        k2w[u] = C["SC"] - 1
        k2e[u] = C["SC"]

    def emit_W(u):
        @block.vector
        def _(ve, u=u):
            pw(ve, ("SC", k2w[u]))
            if u >= 1:
                pw(ve, ("PE", ym[u - 1]))
            ve.tensor_mul(s_wc[:, 0, :], s_we[:, 0, :], s_v0[:, 0, :])
            ve.drain()
            ve.tensor_mul(s_wc[:, 1, :], s_we[:, 1, :], s_v0[:, 1, :])
            inc(ve.drain(), "DV")
        wm[u] = C["DV"]

    def emit_qevac_ymul(u):
        c, h = u // 2, u % 2
        tt0, tt1 = c * tpc, c * tpc + 1

        @block.vector
        def _(ve, u=u, h=h, c=c):
            pw(ve, ("PE", qm[u]))
            if h == 0 and c >= 1:
                pw(ve, ("RSQ", rsqc[c - 1]))
            ve.tensor_copy(s_n16q[:, h, :], p_q[:])
            inc(ve.drain(), "DV")
        qevm[u] = C["DV"]

        if h == 1:
            @block.sync
            def _(sync, u=u, c=c):
                pw(sync, ("DV", qevm[u]))
                inc(sync.dma_start(
                    out=s_qpk[:, c, :],
                    in_=s_n16q[:].rearrange("p h (j f) -> p h j f", j=DT // 2)), "RSQ", 16)
            rsqc[c] = C["RSQ"]

        @block.vector
        def _(ve, u=u, h=h, tt0=tt0, tt1=tt1):
            pw(ve, ("PE", ym[u]))
            pw(ve, ("SC", k2e[u]))
            if u >= 2:
                pw(ve, ("GP", yaddm[u - 2]))
            ve.tensor_mul(s_y[:, u % 2, 0, :], sE[:, tt0, HSL(h)], p_y0[:])
            ve.drain()
            ve.tensor_mul(s_y[:, u % 2, 1, :], sE[:, tt1, HSL(h)], p_y1[:])
            inc(ve.drain(), "DV")
        ymulm[u] = C["DV"]

        @block.gpsimd
        def _(gp, u=u, h=h, tt0=tt0, tt1=tt1):
            pw(gp, ("DV", ymulm[u]))
            for i, tt in enumerate((tt0, tt1)):
                gp.tensor_add(sx[:, tt, HSL(h)], sx[:, tt, HSL(h)], s_y[:, u % 2, i, :])
                gp.drain()
            inc(gp.drain(), "GP")
        yaddm[u] = C["GP"]

    def emit_p16_lp_q_y(u):
        c, h = u // 2, u % 2
        tt0, tt1 = c * tpc, c * tpc + 1

        @block.tensor
        def _(te, u=u):
            if u >= 1:
                pw(te, ("DV", lpcm[u - 1]))
            te.ldweights(s_onb[:, 0:NST])
            for i in range(tpc):
                mm = te.matmul(p_lp[:], s_onb[:, 0:NST], s_da[:, u % 2, i, :],
                               start=(i == 0), stop=(i == tpc - 1))
            inc(mm, "PE")
        p16m[u] = C["PE"]

        @block.vector
        def _(ve, u=u, h=h, c=c):
            pw(ve, ("PE", p16m[u]))
            if h == 0 and c >= 1:
                pw(ve, ("RSL", rslc[c - 1]))
            ve.tensor_copy(s_n16l[:, h, :], p_lp[:])
            inc(ve.drain(), "DV")
        lpcm[u] = C["DV"]

        if h == 1:
            @block.sync
            def _(sync, u=u, c=c):
                pw(sync, ("DV", lpcm[u]))
                inc(sync.dma_start(
                    out=s_lppk[:, c, :],
                    in_=s_n16l[:].rearrange("p h (j f) -> p h j f", j=DT // 2)), "RSL", 16)
            rslc[c] = C["RSL"]

        @block.tensor
        def _(te, u=u, tt0=tt0, tt1=tt1):
            pw(te, ("DV", wm[u]))
            if u >= 1:
                pw(te, ("DV", qevm[u - 1]))
            for i, tt in enumerate((tt0, tt1)):
                te.ldweights(s_btm[:, tt, :])
                mm = te.matmul(p_q[:], s_btm[:, tt, :], s_wc[:, i, :],
                               start=(i == 0), stop=(i == tpc - 1))
            inc(mm, "PE")
        qm[u] = C["PE"]

        @block.tensor
        def _(te, u=u, c=c):
            pw(te, ("DV", wm[u]))
            if u >= 1:
                pw(te, ("DV", ymulm[u - 1]))
            te.ldweights(s_gm[:, c, 0:128])
            te.matmul(p_y0[:], s_gm[:, c, 0:128], s_wc[:, 0, :], start=True, stop=True)
            te.ldweights(s_gm[:, c, 128:256])
            te.matmul(p_y1[:], s_gm[:, c, 128:256], s_wc[:, 0, :], start=True, stop=False)
            te.ldweights(s_gm[:, c, 256:384])
            inc(te.matmul(p_y1[:], s_gm[:, c, 256:384], s_wc[:, 1, :],
                          start=False, stop=True), "PE")
        ym[u] = C["PE"]

    emit_dz(0)
    for u in range(NU):
        c, h = u // 2, u % 2
        tt0, tt1 = c * tpc, c * tpc + 1

        if h == 0:
            @block.gpsimd
            def _(gp, c=c, u=u):
                if c >= 1:
                    pw(gp, ("DV", v0m[u - 1]))
                for i, tt in enumerate((tt0, tt1)):
                    gp.tensor_mul(s_utr[:, i, :], s_u[:, tt, :], s_sig[:])
                    gp.drain()
                inc(gp.drain(), "GP")
            utrm[c] = C["GP"]

        @block.scalar
        def _(sc, u=u, h=h):
            sc.activation(s_dum[:], s_abc[0:1, 0:1], AF.Exp)
            sc.drain()
            pw(sc, ("PE", dzm[u]))
            if u >= 2:
                pw(sc, ("DV", v0m[u - 2]))
            sc.activation(s_delta[:, 0, HSL(h)], p_z0[:], AF.Exp)
            sc.drain()
            sc.activation(s_delta[:, 0, HSL(h)], s_delta[:, 0, HSL(h)], AF.Ln, bias=1.0)
            sc.drain()
            sc.activation(s_delta[:, 1, HSL(h)], p_z1[:], AF.Exp)
            sc.drain()
            sc.activation(s_delta[:, 1, HSL(h)], s_delta[:, 1, HSL(h)], AF.Ln, bias=1.0)
            inc(sc.drain(), "SC")
            if u >= 1:
                up, hp = u - 1, (u - 1) % 2
                tp0, tp1 = (up // 2) * tpc, (up // 2) * tpc + 1
                pw(sc, ("PE", cdm[up]))
                if up >= 1:
                    pw(sc, ("DV", wm[up - 1]))
                sc.activation(s_we[:, 0, :], p_c0[:], AF.Exp, scale=-1.0)
                sc.drain()
                sc.activation(s_we[:, 1, :], p_c1[:], AF.Exp, scale=-1.0)
                inc(sc.drain(), "SC")
                sc.activation(sE[:, tp0, HSL(hp)], p_c0[:], AF.Exp)
                sc.drain()
                sc.activation(sE[:, tp1, HSL(hp)], p_c1[:], AF.Exp)
                inc(sc.drain(), "SC")
        ksp[u] = C["SC"] - (2 if u >= 1 else 0)
        if u >= 1:
            k2w[u - 1] = C["SC"] - 1
            k2e[u - 1] = C["SC"]

        @block.vector
        def _(ve, u=u, h=h):
            pw(ve, ("SC", ksp[u]))
            if u >= 2:
                pw(ve, ("PE", p16m[u - 2]))
            ve.tensor_mul(s_da[:, u % 2, 0, :], s_delta[:, 0, HSL(h)], s_abc[:, HSL(h)])
            ve.drain()
            ve.tensor_mul(s_da[:, u % 2, 1, :], s_delta[:, 1, HSL(h)], s_abc[:, HSL(h)])
            inc(ve.drain(), "DV")
        dam[u] = C["DV"]

        if u >= 1:
            emit_W(u - 1)

        @block.vector
        def _(ve, u=u, h=h, c=c):
            pw(ve, ("GP", utrm[c]))
            ve.tensor_mul(s_v0[:, 0, :], s_delta[:, 0, HSL(h)], s_utr[:, 0, HSL(h)])
            ve.drain()
            ve.tensor_mul(s_v0[:, 1, :], s_delta[:, 1, HSL(h)], s_utr[:, 1, HSL(h)])
            inc(ve.drain(), "DV")
        v0m[u] = C["DV"]

        @block.tensor
        def _(te, u=u):
            pw(te, ("DV", dam[u]))
            if u >= 1:
                pw(te, ("SC", k2e[u - 1]))
            te.ldweights(s_utb[:])
            te.matmul(p_c0[:], s_utb[:], s_da[:, u % 2, 0, :], start=True, stop=True)
            te.ldweights(s_onb[:])
            te.matmul(p_c1[:], s_onb[:], s_da[:, u % 2, 0, :], start=True, stop=False)
            te.ldweights(s_utb[:])
            inc(te.matmul(p_c1[:], s_utb[:], s_da[:, u % 2, 1, :],
                          start=False, stop=True), "PE")
        cdm[u] = C["PE"]

        if u + 1 < NU:
            emit_dz(u + 1)
        if u >= 2:
            emit_qevac_ymul(u - 2)
        if u >= 1:
            emit_p16_lp_q_y(u - 1)

    emit_ew_final(NU - 1)
    emit_W(NU - 1)
    emit_qevac_ymul(NU - 2)
    emit_p16_lp_q_y(NU - 1)
    emit_qevac_ymul(NU - 1)
    esP1.close()

    esF1 = ExitStack()
    s_send = esF1.enter_context(nc.sbuf_tensor("s_send", [128, 2 * BATCH * 128], F32))
    s_pall = esF1.enter_context(nc.sbuf_tensor("s_pall", [128, nch, 128], F32))
    s_fh = esF1.enter_context(nc.sbuf_tensor("s_fh", [128, BATCH, 128], F32))
    s_ft = esF1.enter_context(nc.sbuf_tensor("s_ft", [128, BATCH, 128], F32))
    s_fp = esF1.enter_context(nc.sbuf_tensor("s_fp", [128, BATCH, 128], F32))

    @block.scalar
    def _(sc):
        wt(sc, "RSL")
        wt(sc, "RSQ")
        wt(sc, "DV")
        wt(sc, "GP")
        sc.activation(s_pall[:], s_lppk[:], AF.Exp)
        inc(sc.drain(), "SC")

    @block.vector
    def _(ve):
        wt(ve, "SC")
        ve.tensor_mul(s_fh[:], s_pall[:, 0::ncpb, :], s_qpk[:, 0::ncpb, :])
        ve.drain()
        for lc in range(1, ncpb):
            ve.tensor_add(s_ft[:], s_fh[:], s_qpk[:, lc::ncpb, :])
            ve.drain()
            ve.tensor_mul(s_fh[:], s_ft[:], s_pall[:, lc::ncpb, :])
            ve.drain()
        inc(ve.drain(), "DV")

    @block.vector
    def _(ve):
        for b in range(BATCH):
            ve.tensor_copy(s_send[:, (2 * b) * 128:(2 * b + 1) * 128], s_fh[:, b, :])
        ve.drain()
        ve.tensor_copy(s_ft[:], s_lppk[:, 0::ncpb, :])
        ve.drain()
        for lc in range(1, ncpb):
            ve.tensor_add(s_ft[:], s_ft[:], s_lppk[:, lc::ncpb, :])
            ve.drain()
        for b in range(BATCH):
            ve.tensor_copy(s_send[:, (2 * b + 1) * 128:(2 * b + 2) * 128], s_ft[:, b, :])
        inc(ve.drain(), "DV")

    @block.gpsimd
    def _(gp):
        wt(gp, "DV")
        inc(gp.dma_start(out=cc_in.ap(), in_=s_send[:]), "GW", 16)
        wt(gp, "GW")
        inc(gp.collective_compute(
            "AllGather", ALU.bypass,
            replica_groups=[list(range(NCORES))],
            ins=[cc_in.ap().opt()],
            outs=[cc_out.ap().opt()]), "CC")
    esF1.close()

    ud_mark = {}

    def emit_ud(c):
        @block.gpsimd
        def _(gp, c=c):
            pw(gp, ("GP", yaddm[2 * c + 1]))
            for i in range(tpc):
                tt = c * tpc + i
                gp.tensor_mul(s_gpscr[:], s_u[:, tt, :], s_sigD[:])
                gp.drain()
                gp.tensor_add(sx[:, tt, :], sx[:, tt, :], s_gpscr[:])
                gp.drain()
            inc(gp.drain(), "GP")
        ud_mark[c] = C["GP"]

    emit_ud(0)
    emit_ud(1)

    esF2 = ExitStack()
    s_ag = esF2.enter_context(nc.sbuf_tensor("s_ag", [128, NCORES, 2 * BATCH * 128], F32))
    s_hin = esF2.enter_context(nc.sbuf_tensor("s_hin", [128, BATCH, 128], F32))
    s_h0pk = esF2.enter_context(nc.sbuf_tensor("s_h0pk", [128, nch, 128], F32))
    s_fh2 = esF2.enter_context(nc.sbuf_tensor("s_fh2", [128, BATCH, 128], F32))
    s_ft2 = esF2.enter_context(nc.sbuf_tensor("s_ft2", [128, BATCH, 128], F32))
    s_fp2 = esF2.enter_context(nc.sbuf_tensor("s_fp2", [128, BATCH, 128], F32))
    esF2b = ExitStack()
    s_pall2 = esF2b.enter_context(nc.sbuf_tensor("s_pall2", [128, nch, 128], F32))

    @block.gpsimd
    def _(gp):
        wt(gp, "CC")
        inc(gp.dma_start(out=s_ag[:],
                         in_=cc_out.ap().rearrange("(j p) f -> p j f", p=128)), "GW", 16)

    emit_ud(2)
    emit_ud(3)

    @block.vector
    def _(ve):
        wt(ve, "GW")
        ve.memset(s_hin[:], 0.0)
        inc(ve.drain(), "DV")

    s_ag4 = s_ag[:, :, :].rearrange("p j (k f) -> p j k f", k=2 * BATCH)

    @block.scalar
    def _(sc):
        wt(sc, "DV")
        for j in range(NCORES):
            sc.activation(s_ag4[:, j, 1::2, :], s_ag4[:, j, 1::2, :], AF.Exp,
                          scale=s_mask[:, j:j + 1])
        inc(sc.drain(), "SC")

    @block.vector
    def _(ve):
        wt(ve, "SC")
        for j in range(NCORES):
            ve.tensor_mul(s_ft2[:], s_hin[:], s_ag4[:, j, 1::2, :])
            ve.drain()
            ve.scalar_tensor_tensor(
                s_hin[:], s_ag4[:, j, 0::2, :],
                s_mask[:, j:j + 1], s_ft2[:], ALU.mult, ALU.add)
            ve.drain()
        inc(ve.drain(), "DV")

    @block.scalar
    def _(sc):
        wt(sc, "DV")
        sc.activation(s_pall2[:], s_lppk[:], AF.Exp)
        inc(sc.drain(), "SC")

    @block.vector
    def _(ve):
        wt(ve, "SC")
        ve.tensor_copy(s_fh2[:], s_hin[:])
        ve.drain()
        for lc in range(ncpb):
            ve.tensor_copy(s_h0pk[:, lc::ncpb, :], s_fh2[:])
            ve.drain()
            if lc < ncpb - 1:
                ve.tensor_add(s_ft2[:], s_fh2[:], s_qpk[:, lc::ncpb, :])
                ve.drain()
                ve.tensor_mul(s_fh2[:], s_ft2[:], s_pall2[:, lc::ncpb, :])
                ve.drain()
        inc(ve.drain(), "DV")

    esF2b.close()
    esP2 = ExitStack()
    s_h0bf = esP2.enter_context(nc.sbuf_tensor("s_h0bf", [NST, 3, D], BF16))
    s_scr2 = esP2.enter_context(nc.sbuf_tensor("s_scr2", [128, D], F32))
    p_w0 = esP2.enter_context(nc.psum_tensor("p_w0", [128, D], F32))
    p_w1 = esP2.enter_context(nc.psum_tensor("p_w1", [128, D], F32))

    dv_fold = C["DV"]
    pe_h0use = {}
    rsh_names = ("RSH", "RSH2", "RSH3")
    rsh_marks = {}

    def emit_h0(c):
        nm = rsh_names[c % 3]

        @block.gpsimd
        def _(gp, c=c, nm=nm, dv_fold=dv_fold):
            gp.wait_ge(SEM["DV"], dv_fold)
            if c >= 3:
                pw(gp, ("PE", pe_h0use[c - 3]))
            inc(gp.dma_start(out=s_h0bf[:, c % 3, :].rearrange("p (g f) -> p g f", g=8),
                             in_=s_h0pk[:, c, :]), nm, 16)
        rsh_marks[c] = (nm, C[nm])

    for c in range(3):
        emit_h0(c)

    for c in range(nch):
        t0 = c * tpc
        for i in range(tpc):
            tt = t0 + i
            py = p_w0 if i == 0 else p_w1

            @block.tensor
            def _(te, c=c, tt=tt, py=py):
                pw(te, rsh_marks[c])
                wt(te, "DV")
                te.matmul(py[:, 0:512], sC[:, tt * 128:(tt + 1) * 128],
                          s_h0bf[:, c % 3, 0:512], start=True, stop=True)
                inc(te.matmul(py[:, 512:1024], sC[:, tt * 128:(tt + 1) * 128],
                              s_h0bf[:, c % 3, 512:1024], start=True, stop=True), "PE")
        pe_h0use[c] = C["PE"]

        if c >= 4:
            @block.vector
            def _(ve, t0=t0, c=c):
                if c == 4:
                    pw(ve, ("GP", ud_mark[3]))
                    wt(ve, "GP")
                for tt in (t0, t0 + 1):
                    ve.tensor_mul(s_gpscr[:], s_u[:, tt, :], s_sigD[:])
                    ve.drain()
                    ve.tensor_add(sx[:, tt, :], sx[:, tt, :], s_gpscr[:])
                    ve.drain()
                inc(ve.drain(), "DV")

        for i in range(tpc):
            tt = t0 + i
            py = p_w0 if i == 0 else p_w1

            @block.vector
            def _(ve, tt=tt, py=py, c=c, i=i):
                wt(ve, "PE")
                if i == 0 and c < 4:
                    pw(ve, ("GP", ud_mark[c]))
                ve.tensor_mul(s_scr2[:], sE[:, tt, :], py[:])
                ve.drain()
                ve.tensor_add(sx[:, tt, :], sx[:, tt, :], s_scr2[:])
                inc(ve.drain(), "DV")

        if c + 3 < nch:
            emit_h0(c + 3)
    esP2.close()
    esF2.close()
    esDB.close()
    esU.close()
    esM.close()

    esFN = ExitStack()
    s_x2t = esFN.enter_context(nc.sbuf_tensor("s_x2t", [128, DT, ntok], BF16))
    s_hq = esFN.enter_context(nc.sbuf_tensor("s_hq", [128, HT, ntok], FP8))
    s_bq = esFN.enter_context(nc.sbuf_tensor("s_bq", [128, D], F16))
    s_w2q = esFN.enter_context(nc.sbuf_tensor("s_w2q", [128, HT, D], FP8))

    @block.gpsimd
    def _(gp):
        wt(gp, "DV")
        inc(gp.dma_start(out=s_bq[:], in_=b2bc), "GW", 16)
        for h in range(HT):
            inc(gp.dma_start(out=s_w2q[:, h, :], in_=w2t[h, :, :]), "GW", 16)


    @block.vector
    def _(ve):
        wt(ve, "GP")
        inc(ve.drain(), "DV")

    w1cnt = {"a": 0, "b": 0}
    ntb_h = htok // 512
    w1cnt = {"a": 0, "b": 0}
    ntb_h = htok // 512

    ptb_free = [(None, 0), (None, 0)]

    def emit_T_grp(tbg, sc_ln, p_tbA, p_tbB):
        for g in range(DT):
            par = g % 2
            ptr = p_tbA if par == 0 else p_tbB

            @block.tensor
            def _(te, g=g, sc_ln=sc_ln, ptbf=ptb_free[par], ptr=ptr):
                te.wait_ge(SEM["SC"], sc_ln)
                pw(te, ptbf)
                for q in range(4):
                    inc(te.transpose(ptr[:, q * 128:(q + 1) * 128],
                                     s_x2[:, q, g * 128:(g + 1) * 128],
                                     s_idb[:]), "PE")
            pe_t = C["PE"]

            @block.vector
            def _(ve, tbg=tbg, g=g, pe_t=pe_t, ptr=ptr):
                ve.wait_ge(SEM["PE"], pe_t)
                ve.tensor_copy(s_x2t[:, g, tbg * 512:(tbg + 1) * 512], ptr[:])
                inc(ve.drain(), "DV")
            ptb_free[par] = ("DV", C["DV"])
        return C["PE"]

    def emit_m1_all():
        esM1 = ExitStack()
        s_w1a = esM1.enter_context(nc.sbuf_tensor("s_w1a", [128, DT, 128], BF16))
        s_w1b = esM1.enter_context(nc.sbuf_tensor("s_w1b", [128, DT, 128], BF16))
        psums = [esM1.enter_context(nc.psum_tensor("p_m1_%d" % i, [128, ntok], F32))
                 for i in range(2)]
        evac_m = {}

        @block.gpsimd
        def _(gp):
            wt(gp, "DV")
            wt(gp, "PE")
            inc(gp.dma_start(out=s_w1a[:], in_=w1t[0, :, :, :]), "W1a", 16)
            w1cnt["a"] += 1

        @block.gpsimd
        def _(gp):
            inc(gp.dma_start(out=s_w1b[:], in_=w1t[1, :, :, :]), "W1b", 16)
            w1cnt["b"] += 1
        gpw1_done = C["W1b"]
        dv_x2t = C["DV"]
        ntb = ntok // 512
        pe_marks = {}
        for h in range(HT):
            par = "a" if h % 2 == 0 else "b"
            pm = psums[h % 2]
            slab = s_w1a if h % 2 == 0 else s_w1b

            @block.tensor
            def _(te, h=h, pm=pm, slab=slab, par=par, cnt=w1cnt[par], dv_x2t=dv_x2t):
                te.wait_ge(SEM["DV"], dv_x2t)
                te.wait_ge(SEM["W1" + par], 16 * cnt)
                if h >= 2:
                    pw(te, ("SC", evac_m[h - 2]))
                for g in range(DT):
                    te.ldweights(slab[:, g, :])
                    for tb in range(ntb):
                        mm = te.matmul(pm[:, tb * 512:(tb + 1) * 512], slab[:, g, :],
                                       s_x2t[:, g, tb * 512:(tb + 1) * 512],
                                       start=(g == 0), stop=(g == DT - 1))
                        if g == DT - 1 and tb == ntb - 1:
                            inc(mm, "PE")
            pe_marks[h] = C["PE"]

            if h + 2 < HT:
                if h % 2 == 0:
                    @block.gpsimd
                    def _(gp, h=h, pe_h=pe_marks[h]):
                        gp.wait_ge(SEM["PE"], pe_h)
                        inc(gp.dma_start(out=s_w1a[:], in_=w1t[h + 2, :, :, :]), "W1a", 16)
                    w1cnt["a"] += 1
                else:
                    @block.gpsimd
                    def _(gp, h=h, pe_h=pe_marks[h]):
                        gp.wait_ge(SEM["PE"], pe_h)
                        inc(gp.dma_start(out=s_w1b[:], in_=w1t[h + 2, :, :, :]), "W1b", 16)
                    w1cnt["b"] += 1

            @block.scalar
            def _(sc, h=h, pm=pm, pe_h=pe_marks[h]):
                sc.wait_ge(SEM["PE"], pe_h)
                for tb in range(ntb):
                    sc.activation(s_hq[:, h, tb * 512:(tb + 1) * 512],
                                  pm[:, tb * 512:(tb + 1) * 512], AF.Relu,
                                  bias=s_b1[:, h:h + 1])
                inc(sc.drain(), "SC")
            evac_m[h] = C["SC"]
        esM1.close()
        return C["SC"]

    def emit_m2(half, m1_sc):
        tbase = half * hnt
        esM2 = ExitStack()
        s_tf16 = esM2.enter_context(nc.sbuf_tensor("s_tf16_%d" % half, [128, D], F16))
        pms = [esM2.enter_context(nc.psum_tensor("p_m2_%d_%d" % (half, i), [128, D], F32))
               for i in range(2)]
        gw_w2 = C["GW"]
        m2_evac = [C["DV"], C["DV"]]
        for t in range(hnt):
            pm = pms[t % 2]

            @block.tensor
            def _(te, t=t, pm=pm, ev=m2_evac[t % 2], gw_w2=gw_w2, m1_sc=m1_sc, tbase=tbase):
                te.wait_ge(SEM["SC"], m1_sc)
                te.wait_ge(SEM["GW"], gw_w2)
                te.wait_ge(SEM["DV"], ev)
                DR = mybir.MatmulPerfMode.DoubleRow
                for hp in range(0, HT, 2):
                    lhs = s_hq[:, hp:hp + 2, (tbase + t) * 128:(tbase + t + 1) * 128]
                    te.ldweights(lhs, perf_mode=DR)
                    mm = te.matmul(pm[:, 0:512], lhs, s_w2q[:, hp:hp + 2, 0:512],
                                   start=(hp == 0), stop=(hp == HT - 2), perf_mode=DR)
                    mm = te.matmul(pm[:, 512:1024], lhs, s_w2q[:, hp:hp + 2, 512:1024],
                                   start=(hp == 0), stop=(hp == HT - 2), perf_mode=DR)
                    if hp == HT - 2:
                        inc(mm, "PE")
            pe_t = C["PE"]

            @block.vector
            def _(ve, t=t, pm=pm, pe_t=pe_t, tbase=tbase):
                ve.wait_ge(SEM["PE"], pe_t)
                ve.tensor_add(s_tf16[:], pm[:], s_bq[:])
                ve.drain()
                ve.tensor_add(sx[:, tbase + t, :], sx[:, tbase + t, :], s_tf16[:])
                inc(ve.drain(), "DV")
            m2_evac[t % 2] = C["DV"]
        esM2.close()

    esX2 = ExitStack()
    s_x2 = esX2.enter_context(nc.sbuf_tensor("s_x2", [128, 4, D], BF16))
    esT0 = ExitStack()
    p_tbA0 = esT0.enter_context(nc.psum_tensor("p_tbA0", [128, 512], BF16))
    p_tbB0 = esT0.enter_context(nc.psum_tensor("p_tbB0", [128, 512], BF16))
    ngrpT = nt // 4
    sc_ffn_base = C["SC"]
    prev_pe = 0
    for tbg in range(ngrpT):
        sc_ln = emit_ln_range([tbg * 4 + q for q in range(4)],
                              lambda k, t: s_x2[:, k, :],
                              first_wait=(None if tbg == 0 else ("SC", sc_ffn_base)),
                              norm_wait=prev_pe, idx0=tbg * 4)
        prev_pe = emit_T_grp(tbg, sc_ln, p_tbA0, p_tbB0)
    esT0.close()
    esX2.close()

    m1_sc = emit_m1_all()
    emit_m2(0, m1_sc)
    emit_m2(1, m1_sc)
    esFN.close()

    @block.sync
    def _(sync):
        wt(sync, "DV")
        wt(sync, "GP")
        for t in range(nt):
            inc(sync.dma_start(out=y_out[t * 128:(t + 1) * 128, :], in_=sx[:, t, :]), "ST", 16)
        sync.wait_ge(SEM["ST"], C["ST"])

    es.close()
    return nc


def _host_prep(L0_attn_x, L0_attn_A_log, L0_attn_x_proj_w, L0_attn_dt_proj_w,
               L0_attn_dt_proj_b, L0_attn_D_param, L0_attn_trend_emphasis,
               L0_ffn_w1, L0_ffn_b1, L0_ffn_w2, L0_ffn_b2, lcore):
    f = np.ascontiguousarray
    x = np.asarray(L0_attn_x, dtype=np.float32)
    w1 = np.asarray(L0_ffn_w1, dtype=np.float32)
    w2 = np.asarray(L0_ffn_w2, dtype=np.float32)
    xpw = np.asarray(L0_attn_x_proj_w, dtype=np.float32)
    shared = {
        "w1t": f(w1.T.reshape(DT, 128, HT, 128).transpose(2, 1, 0, 3)),
        "w2t": f(w2.T.reshape(HT, 128, D)),
        "b1t": f(np.asarray(L0_ffn_b1, np.float32).reshape(HT, 128).T),
        "b2bc": f(np.broadcast_to(np.asarray(L0_ffn_b2, np.float32), (128, D))),
        "tebc": f(np.broadcast_to(np.asarray(L0_attn_trend_emphasis, np.float32), (128, D))),
        "dbc": f(np.broadcast_to(np.asarray(L0_attn_D_param, np.float32), (128, D))),
        "alogT": f(np.asarray(L0_attn_A_log, np.float32).T),
        "xpwd": f(xpw[:RANK].T.reshape(DT, 128, RANK).transpose(1, 0, 2)),
        "xpwb": f(xpw[RANK:RANK + NST].T.reshape(DT, 128, NST).transpose(1, 0, 2)),
        "xpwc": f(xpw[RANK + NST:].T.reshape(DT, 128, NST).transpose(1, 0, 2)),
        "dtwtb": f(np.concatenate([np.asarray(L0_attn_dt_proj_w, np.float32).T,
                                   np.asarray(L0_attn_dt_proj_b, np.float32)[None, :]], axis=0)),
        "identf": np.eye(128, dtype=np.float32),
        "tet": f(np.asarray(L0_attn_trend_emphasis, np.float32).reshape(DT, 128).T),
        "utf": f(np.triu(np.ones((128, 128), dtype=np.float32))),
    }
    in_maps = []
    for k in range(NCORES):
        m = dict(shared)
        m["x_in"] = f(x[:, k * lcore:(k + 1) * lcore, :].reshape(BATCH * lcore, D))
        mask = np.zeros((128, NCORES), dtype=np.float32)
        mask[:, :k] = 1.0
        m["maskin"] = mask
        in_maps.append(m)
    return in_maps


def kernel(**inputs):
    lcore = np.asarray(inputs["L0_attn_x"]).shape[1] // NCORES
    in_maps = _host_prep(lcore=lcore, **inputs)
    nc = build(lcore=lcore)
    res = run_bass_kernel_spmd(nc, in_maps, core_ids=list(range(NCORES)))
    out = np.empty((BATCH, NCORES * lcore, D), dtype=np.float32)
    for k in range(NCORES):
        y = np.asarray(res.results[k]["y_out"]).reshape(BATCH, lcore, D)
        out[:, k * lcore:(k + 1) * lcore, :] = y
    return out


# version 33
# speedup vs baseline: 1.0282x; 1.0282x over previous
import numpy as np
import concourse.bass as bass
import concourse.mybir as mybir
from concourse.bass_utils import run_bass_kernel_spmd
from contextlib import ExitStack

F32 = mybir.dt.float32
BF16 = mybir.dt.bfloat16
F16 = mybir.dt.float16
FP8 = mybir.dt.float8e4
AF = mybir.ActivationFunctionType
ALU = mybir.AluOpType
AXX = mybir.AxisListType.X

NCORES = 8
BATCH = 2
D = 1024
H = 4096
RANK = 64
NST = 16
DT = D // 128
HT = H // 128
CH = 256
EPS = 1e-5


def build(lcore=1024):
    ntok = BATCH * lcore
    nt = ntok // 128
    ncpb = lcore // CH
    nch = BATCH * ncpb
    tpc = CH // 128
    nhalf = 2
    htok = ntok // nhalf
    hnt = nt // nhalf

    nc = bass.Bass("TRN2", target_bir_lowering=False, debug=False)

    x_in = nc.dram_tensor("x_in", [ntok, D], F32, kind="ExternalInput").ap()
    w1t = nc.dram_tensor("w1t", [HT, 128, DT, 128], F32, kind="ExternalInput").ap()
    w2t = nc.dram_tensor("w2t", [HT, 128, D], F32, kind="ExternalInput").ap()
    b1t = nc.dram_tensor("b1t", [128, HT], F32, kind="ExternalInput").ap()
    b2bc = nc.dram_tensor("b2bc", [128, D], F32, kind="ExternalInput").ap()
    tebc = nc.dram_tensor("tebc", [128, D], F32, kind="ExternalInput").ap()
    dbc = nc.dram_tensor("dbc", [128, D], F32, kind="ExternalInput").ap()
    alogT = nc.dram_tensor("alogT", [NST, D], F32, kind="ExternalInput").ap()
    xpwd = nc.dram_tensor("xpwd", [128, DT, RANK], F32, kind="ExternalInput").ap()
    xpwb = nc.dram_tensor("xpwb", [128, DT, NST], F32, kind="ExternalInput").ap()
    xpwc = nc.dram_tensor("xpwc", [128, DT, NST], F32, kind="ExternalInput").ap()
    dtwtb = nc.dram_tensor("dtwtb", [RANK + 1, D], F32, kind="ExternalInput").ap()
    identf = nc.dram_tensor("identf", [128, 128], F32, kind="ExternalInput").ap()
    utf = nc.dram_tensor("utf", [128, 128], F32, kind="ExternalInput").ap()
    maskin = nc.dram_tensor("maskin", [128, NCORES], F32, kind="ExternalInput").ap()
    tet = nc.dram_tensor("tet", [128, DT], F32, kind="ExternalInput").ap()
    y_out = nc.dram_tensor("y_out", [ntok, D], F32, kind="ExternalOutput").ap()

    cc_in = nc.dram_tensor("cc_in", [128, 2 * BATCH * 128], F32)
    cc_out = nc.dram_tensor("cc_out", [128 * NCORES, 2 * BATCH * 128], F32, addr_space="Shared")

    es = ExitStack()
    ec = es.enter_context

    sx = ec(nc.sbuf_tensor("sx", [128, nt, D], F32))
    s_idb = ec(nc.sbuf_tensor("s_idb", [128, 128], BF16))
    s_utb = ec(nc.sbuf_tensor("s_utb", [128, 128], BF16))
    s_onb = ec(nc.sbuf_tensor("s_onb", [128, 128], BF16))
    s_b1 = ec(nc.sbuf_tensor("s_b1", [128, HT], F32))
    s_xpwd = ec(nc.sbuf_tensor("s_xpwd", [128, DT, RANK], BF16))
    s_xpwb = ec(nc.sbuf_tensor("s_xpwb", [128, DT, NST], BF16))
    s_xpwc = ec(nc.sbuf_tensor("s_xpwc", [128, DT, NST], BF16))
    s_mask = ec(nc.sbuf_tensor("s_mask", [128, NCORES], F32))
    s_sigc = ec(nc.sbuf_tensor("s_sigc", [128, DT], F32))
    s_bn = ec(nc.sbuf_tensor("s_bn", [128, 12], F32))
    s_mv = ec(nc.sbuf_tensor("s_mv", [128, 32], F32))
    s_rstd = ec(nc.sbuf_tensor("s_rstd", [128, 16], F32))
    s_mrs = ec(nc.sbuf_tensor("s_mrs", [128, 16], F32))

    names = ("LD", "LA", "LB", "LC", "GW", "W1a", "W1b", "RSL", "RSQ", "RSH", "RSH2", "RSH3", "RSH4", "XA", "XB", "XC", "XD", "PE", "SC", "DV", "GP", "CC", "ST")
    SEM = {n: ec(nc.semaphore("s" + n)) for n in names}
    C = {n: 0 for n in names}

    def inc(instr, name, n=1):
        C[name] += n
        instr.then_inc(SEM[name], n)

    def wt(eng, name):
        if C[name] > 0:
            eng.wait_ge(SEM[name], C[name])

    def pw(eng, dep):
        if dep[0] is not None and dep[1] > 0:
            eng.wait_ge(SEM[dep[0]], dep[1])

    block = ec(nc.Block())

    xldm = {}

    @block.sync
    def _(sync):
        inc(sync.dma_start(out=s_b1[:], in_=b1t), "LD", 16)
        inc(sync.dma_start(out=s_mask[:], in_=maskin), "LD", 16)
        inc(sync.dma_start(out=s_sigc[:], in_=tet), "LD", 16)

    @block.gpsimd
    def _(gp):
        inc(gp.dma_start(out=s_idb[:], in_=identf), "GW", 16)
        inc(gp.dma_start(out=s_utb[:], in_=utf), "GW", 16)
        inc(gp.dma_start(out=s_xpwd[:], in_=xpwd), "GW", 16)
        inc(gp.dma_start(out=s_xpwb[:], in_=xpwb), "GW", 16)
        inc(gp.dma_start(out=s_xpwc[:], in_=xpwc), "GW", 16)

    xsem = {0: "XA", 2: "LB", 1: "LA", 3: "XB"}
    for t in range(4):
        nm = xsem.get(t, "XD" if t % 2 == 0 else "XC")
        if t % 2 == 0:
            @block.sync
            def _(sync, t=t, nm=nm):
                inc(sync.dma_start(out=sx[:, t, :], in_=x_in[t * 128:(t + 1) * 128, :]), nm, 16)
        else:
            @block.gpsimd
            def _(gp, t=t, nm=nm):
                inc(gp.dma_start(out=sx[:, t, :], in_=x_in[t * 128:(t + 1) * 128, :]), nm, 16)
        xldm[t] = (nm, C[nm])

    @block.vector
    def _(ve):
        wt(ve, "LD")
        ve.memset(s_onb[:], 1.0)
        inc(ve.drain(), "DV")
    dv_cst = C["DV"]

    @block.scalar
    def _(sc, dv_cst=dv_cst):
        sc.wait_ge(SEM["DV"], dv_cst)
        sc.activation(s_sigc[:], s_sigc[:], AF.Sigmoid)
        inc(sc.drain(), "SC")
    sc_sigc = C["SC"]

    @block.vector
    def _(ve, sc_sigc=sc_sigc):
        ve.wait_ge(SEM["SC"], sc_sigc)
        wt(ve, "GW")
        for g in range(DT):
            ve.tensor_scalar_mul(s_xpwd[:, g, :], s_xpwd[:, g, :], s_sigc[:, g:g + 1])
            ve.drain()
            ve.tensor_scalar_mul(s_xpwb[:, g, :], s_xpwb[:, g, :], s_sigc[:, g:g + 1])
            ve.drain()
            ve.tensor_scalar_mul(s_xpwc[:, g, :], s_xpwc[:, g, :], s_sigc[:, g:g + 1])
            ve.drain()
        inc(ve.drain(), "DV")

    def emit_ln_range(tiles, out_ap_fn, first_wait=None, norm_wait=0, idx0=0):
        ntl = len(tiles)
        i0 = idx0
        @block.vector
        def _(ve):
            if first_wait is None:
                wt(ve, "SC")
            else:
                pw(ve, first_wait)
            for k, t in enumerate(tiles):
                ve.bn_stats(s_bn[:, 0:6], sx[:, t, 0:512])
                ve.bn_stats(s_bn[:, 6:12], sx[:, t, 512:1024])
                ve.drain()
                ve.bn_aggr(s_mv[:, 2 * (i0 + k):2 * (i0 + k) + 2], s_bn[:])
                ve.drain()
            for k in range(ntl):
                ve.tensor_scalar_add(s_rstd[:, i0 + k:i0 + k + 1], s_mv[:, 2 * (i0 + k) + 1:2 * (i0 + k) + 2], EPS)
                ve.drain()
            inc(ve.drain(), "DV")

        @block.scalar
        def _(sc):
            wt(sc, "DV")
            sc.activation(s_rstd[:, i0:i0 + ntl], s_rstd[:, i0:i0 + ntl], AF.Sqrt)
            inc(sc.drain(), "SC")

        @block.vector
        def _(ve):
            wt(ve, "SC")
            ve.reciprocal(s_rstd[:, i0:i0 + ntl], s_rstd[:, i0:i0 + ntl])
            ve.drain()
            for k in range(ntl):
                ve.tensor_mul(s_mrs[:, i0 + k:i0 + k + 1], s_mv[:, 2 * (i0 + k):2 * (i0 + k) + 1], s_rstd[:, i0 + k:i0 + k + 1])
                ve.drain()
            ve.tensor_scalar_mul(s_mrs[:, i0:i0 + ntl], s_mrs[:, i0:i0 + ntl], -1.0)
            inc(ve.drain(), "DV")

        @block.scalar
        def _(sc, norm_wait=norm_wait):
            wt(sc, "DV")
            if norm_wait > 0:
                sc.wait_ge(SEM["PE"], norm_wait)
            for k, t in enumerate(tiles):
                sc.activation(out_ap_fn(k, t), sx[:, t, :], AF.Identity,
                              bias=s_mrs[:, i0 + k:i0 + k + 1], scale=s_rstd[:, i0 + k:i0 + k + 1])
            inc(sc.drain(), "SC")
        return C["SC"]

    esM = ExitStack()
    s_abc = esM.enter_context(nc.sbuf_tensor("s_abc", [128, D], BF16))
    sE = esM.enter_context(nc.sbuf_tensor("sE", [128, nt, D], BF16))
    s_dp = esM.enter_context(nc.sbuf_tensor("s_dp", [RANK + 1, ntok], BF16))
    sC = esM.enter_context(nc.sbuf_tensor("sC", [NST, ntok], BF16))
    s_qpk = esM.enter_context(nc.sbuf_tensor("s_qpk", [128, nch, 128], F32))
    s_btm = esM.enter_context(nc.sbuf_tensor("s_btm", [128, nt, NST], BF16))
    s_gm = esM.enter_context(nc.sbuf_tensor("s_gm", [128, nch, 3 * 128], BF16))
    s_lppk = esM.enter_context(nc.sbuf_tensor("s_lppk", [128, nch, 128], F32))
    s_dtw = esM.enter_context(nc.sbuf_tensor("s_dtw", [RANK + 1, D], BF16))
    esU = ExitStack()
    s_u = esU.enter_context(nc.sbuf_tensor("s_u", [128, nt, D], BF16))

    esA = ExitStack()
    s_alT = esA.enter_context(nc.sbuf_tensor("s_alT", [NST, D], F32))
    s_on16 = esA.enter_context(nc.sbuf_tensor("s_on16", [NST, 128], F32))
    p_bc = esA.enter_context(nc.psum_tensor("p_bc", [128, D], F32))


    @block.sync
    def _(sync):
        inc(sync.dma_start(out=s_alT[:], in_=alogT), "LC", 16)

    @block.gpsimd
    def _(gp):
        inc(gp.dma_start(out=s_dtw[:], in_=dtwtb), "XC", 16)

    for t in range(4, nt):
        nm = xsem.get(t, "XD" if t % 2 == 0 else "XC")
        if t % 2 == 0:
            @block.sync
            def _(sync, t=t, nm=nm):
                inc(sync.dma_start(out=sx[:, t, :], in_=x_in[t * 128:(t + 1) * 128, :]), nm, 16)
        else:
            @block.gpsimd
            def _(gp, t=t, nm=nm):
                inc(gp.dma_start(out=sx[:, t, :], in_=x_in[t * 128:(t + 1) * 128, :]), nm, 16)
        xldm[t] = (nm, C[nm])

    @block.vector
    def _(ve):
        wt(ve, "LC")
        ve.memset(s_on16[:], 1.0)
        inc(ve.drain(), "DV")

    @block.tensor
    def _(te):
        wt(te, "DV")
        te.matmul(p_bc[:, 0:512], s_on16[:], s_alT[:, 0:512], start=True, stop=True)
        inc(te.matmul(p_bc[:, 512:1024], s_on16[:], s_alT[:, 512:1024],
                      start=True, stop=True), "PE")

    @block.scalar
    def _(sc):
        wt(sc, "PE")
        sc.activation(s_abc[:], p_bc[:], AF.Exp, scale=1.0 / NST)
        inc(sc.drain(), "SC")

    @block.vector
    def _(ve):
        wt(ve, "SC")
        ve.tensor_scalar_mul(s_abc[:], s_abc[:], -1.0)
        inc(ve.drain(), "DV")
    esA.close()

    esPA = ExitStack()
    esDB = ExitStack()
    s_dbc = esDB.enter_context(nc.sbuf_tensor("s_dbc", [128, D], BF16))
    s_sig = esDB.enter_context(nc.sbuf_tensor("s_sig", [128, D], BF16))
    s_sigD = esDB.enter_context(nc.sbuf_tensor("s_sigD", [128, D], BF16))
    s_gpscr = esDB.enter_context(nc.sbuf_tensor("s_gpscr", [128, D], F32))
    esB = ExitStack()
    sB = esB.enter_context(nc.sbuf_tensor("sB", [NST, ntok], BF16))
    s_ut = esPA.enter_context(nc.sbuf_tensor("s_ut", [128, DT, 512], BF16))
    p_t = esPA.enter_context(nc.psum_tensor("p_t", [128, 512], BF16))
    p_t2 = esPA.enter_context(nc.psum_tensor("p_t2", [128, 512], BF16))
    p_d = esPA.enter_context(nc.psum_tensor("p_d", [RANK, 512], F32))
    p_b = esPA.enter_context(nc.psum_tensor("p_b", [NST, 512], F32))
    p_c = esPA.enter_context(nc.psum_tensor("p_c", [NST, 512], F32))
    p_g = esPA.enter_context(nc.psum_tensor("p_g", [128, 512], F32))

    @block.gpsimd
    def _(gp):
        wt(gp, "SC")
        wt(gp, "DV")
        inc(gp.dma_start(out=s_sig[:], in_=tebc), "GW", 16)
        inc(gp.dma_start(out=s_dbc[:], in_=dbc), "GW", 16)
    gw_sig = C["GW"]

    @block.scalar
    def _(sc, gw_sig=gw_sig):
        sc.wait_ge(SEM["GW"], gw_sig)
        sc.activation(s_sig[:], s_sig[:], AF.Sigmoid)
        inc(sc.drain(), "SC")
    sc_sig = C["SC"]

    @block.vector
    def _(ve):
        wt(ve, "SC")
        ve.memset(s_dp[RANK:RANK + 1, :], 1.0)
        inc(ve.drain(), "DV")

    ngrp = nt // 4
    pt_free = [(None, 0), (None, 0)]
    pdbc_free = (None, 0)
    for g4 in range(ngrp):
        tiles = [g4 * 4 + q for q in range(4)]

        @block.vector
        def _(ve, tiles=tiles, g4=g4):
            if g4 == 0:
                wt(ve, "SC")
                for t in tiles:
                    pw(ve, xldm[t])
            else:
                wt(ve, "XD")
                wt(ve, "XC")
            for t in tiles:
                ve.bn_stats(s_bn[:, 0:6], sx[:, t, 0:512])
                ve.bn_stats(s_bn[:, 6:12], sx[:, t, 512:1024])
                ve.drain()
                ve.bn_aggr(s_mv[:, 2 * t:2 * t + 2], s_bn[:])
                ve.drain()
                ve.tensor_scalar_add(s_rstd[:, t:t + 1], s_mv[:, 2 * t + 1:2 * t + 2], EPS)
                ve.drain()
            inc(ve.drain(), "DV")
        dv_st = C["DV"]

        @block.scalar
        def _(sc, tiles=tiles, dv_st=dv_st):
            sc.wait_ge(SEM["DV"], dv_st)
            sc.activation(s_rstd[:, tiles[0]:tiles[0] + 4], s_rstd[:, tiles[0]:tiles[0] + 4], AF.Sqrt)
            inc(sc.drain(), "SC")
        sc_sq = C["SC"]

        @block.vector
        def _(ve, tiles=tiles, sc_sq=sc_sq):
            ve.wait_ge(SEM["SC"], sc_sq)
            t0g = tiles[0]
            ve.reciprocal(s_rstd[:, t0g:t0g + 4], s_rstd[:, t0g:t0g + 4])
            ve.drain()
            for t in tiles:
                ve.tensor_mul(s_mrs[:, t:t + 1], s_mv[:, 2 * t:2 * t + 1], s_rstd[:, t:t + 1])
                ve.drain()
            ve.tensor_scalar_mul(s_mrs[:, t0g:t0g + 4], s_mrs[:, t0g:t0g + 4], -1.0)
            inc(ve.drain(), "DV")
        dv_ms = C["DV"]

        @block.scalar
        def _(sc, tiles=tiles, dv_ms=dv_ms):
            sc.wait_ge(SEM["DV"], dv_ms)
            for t in tiles:
                sc.activation(s_u[:, t, :], sx[:, t, :], AF.Identity,
                              bias=s_mrs[:, t:t + 1], scale=s_rstd[:, t:t + 1])
            inc(sc.drain(), "SC")
        sc_no = C["SC"]

        gp_us = None

        for g in range(DT):
            par = g % 2
            ptr = p_t if par == 0 else p_t2

            @block.tensor
            def _(te, tiles=tiles, g=g, sc_no=sc_no, ptf=pt_free[g % 2], ptr=ptr):
                te.wait_ge(SEM["SC"], sc_no)
                pw(te, ptf)
                for q in range(4):
                    inc(te.transpose(ptr[:, q * 128:(q + 1) * 128],
                                     s_u[:, tiles[q], g * 128:(g + 1) * 128],
                                     s_idb[:]), "PE")
            pe_t = C["PE"]

            @block.vector
            def _(ve, g=g, pe_t=pe_t, ptr=ptr):
                ve.wait_ge(SEM["PE"], pe_t)
                ve.tensor_copy(s_ut[:, g, :], ptr[:])
                inc(ve.drain(), "DV")
            pt_free[par] = ("DV", C["DV"])
        dv_ut = C["DV"]

        @block.tensor
        def _(te, g4=g4, dv_ut=dv_ut, pdbcf=pdbc_free):
            te.wait_ge(SEM["DV"], dv_ut)
            pw(te, pdbcf)
            for g in range(DT):
                te.ldweights(s_xpwd[:, g, :])
                te.matmul(p_d[:], s_xpwd[:, g, :], s_ut[:, g, :],
                          start=(g == 0), stop=(g == DT - 1))
            for g in range(DT):
                te.ldweights(s_xpwb[:, g, :])
                te.matmul(p_b[:], s_xpwb[:, g, :], s_ut[:, g, :],
                          start=(g == 0), stop=(g == DT - 1))
            for g in range(DT):
                te.ldweights(s_xpwc[:, g, :])
                mm = te.matmul(p_c[:], s_xpwc[:, g, :], s_ut[:, g, :],
                               start=(g == 0), stop=(g == DT - 1))
            inc(mm, "PE")
        pe_xz = C["PE"]

        @block.scalar
        def _(sc, g4=g4, pe_xz=pe_xz):
            sc.wait_ge(SEM["PE"], pe_xz)
            sc.activation(s_dp[0:RANK, g4 * 512:(g4 + 1) * 512], p_d[:], AF.Identity)
            sc.activation(sB[:, g4 * 512:(g4 + 1) * 512], p_b[:], AF.Identity)
            sc.activation(sC[:, g4 * 512:(g4 + 1) * 512], p_c[:], AF.Identity)
            inc(sc.drain(), "SC")
        pdbc_free = ("SC", C["SC"])

    @block.tensor
    def _(te):
        wt(te, "SC")
        pw(te, pt_free[0])
        for tt in range(nt):
            inc(te.transpose(p_t[:, (tt % 32) * NST:((tt % 32) + 1) * NST],
                             sB[:, tt * 128:(tt + 1) * 128],
                             s_idb[0:NST, 0:NST]), "PE")

    @block.vector
    def _(ve):
        wt(ve, "PE")
        ve.tensor_copy(s_btm[:], p_t[:, 0:nt * NST])
        inc(ve.drain(), "DV")

    gmf = (None, 0)
    for c in range(nch):
        t0 = c * tpc

        @block.tensor
        def _(te, c=c, t0=t0, gmf=gmf):
            if c == 0:
                wt(te, "DV")
            pw(te, gmf)
            inc(te.matmul(p_g[:, 0:128], sB[:, t0 * 128:(t0 + 1) * 128],
                          sC[:, t0 * 128:(t0 + 1) * 128], start=True, stop=True), "PE")
            inc(te.matmul(p_g[:, 128:256], sB[:, t0 * 128:(t0 + 1) * 128],
                          sC[:, (t0 + 1) * 128:(t0 + 2) * 128], start=True, stop=True), "PE")
            inc(te.matmul(p_g[:, 256:384], sB[:, (t0 + 1) * 128:(t0 + 2) * 128],
                          sC[:, (t0 + 1) * 128:(t0 + 2) * 128], start=True, stop=True), "PE")
        pe_gm = C["PE"]

        @block.vector
        def _(ve, c=c, pe_gm=pe_gm):
            ve.wait_ge(SEM["PE"], pe_gm)
            ve.tensor_mul(s_gm[:, c, 0:128], p_g[:, 0:128], s_utb[:])
            ve.tensor_copy(s_gm[:, c, 128:256], p_g[:, 128:256])
            ve.tensor_mul(s_gm[:, c, 256:384], p_g[:, 256:384], s_utb[:])
            inc(ve.drain(), "DV")
        gmf = ("DV", C["DV"])
    esPA.close()
    esB.close()

    esP1 = ExitStack()
    s_delta = esP1.enter_context(nc.sbuf_tensor("s_delta", [128, tpc, D], F32))
    s_we = esP1.enter_context(nc.sbuf_tensor("s_we", [128, tpc, 512], BF16))
    s_da = esP1.enter_context(nc.sbuf_tensor("s_da", [128, 2, tpc, 512], BF16))
    s_v0 = esP1.enter_context(nc.sbuf_tensor("s_v0", [128, tpc, 512], BF16))
    s_wc = esP1.enter_context(nc.sbuf_tensor("s_wc", [128, tpc, 512], BF16))
    s_y = esP1.enter_context(nc.sbuf_tensor("s_y", [128, 2, tpc, 512], BF16))
    s_n16l = esP1.enter_context(nc.sbuf_tensor("s_n16l", [NST, 2, 512], F32))
    s_n16q = esP1.enter_context(nc.sbuf_tensor("s_n16q", [NST, 2, 512], F32))
    s_dum = esP1.enter_context(nc.sbuf_tensor("s_dum", [1, 1], F32))
    p_z0 = esP1.enter_context(nc.psum_tensor("p_z0", [128, 512], F32))
    p_z1 = esP1.enter_context(nc.psum_tensor("p_z1", [128, 512], F32))
    p_c0 = esP1.enter_context(nc.psum_tensor("p_c0", [128, 512], F32))
    p_c1 = esP1.enter_context(nc.psum_tensor("p_c1", [128, 512], F32))
    p_y0 = esP1.enter_context(nc.psum_tensor("p_y0", [128, 512], F32))
    p_y1 = esP1.enter_context(nc.psum_tensor("p_y1", [128, 512], F32))
    p_lp = esP1.enter_context(nc.psum_tensor("p_lp", [NST, 512], F32))
    p_q = esP1.enter_context(nc.psum_tensor("p_q", [NST, 512], F32))

    sc_done = C["SC"]
    dv_done = C["DV"]

    @block.gpsimd
    def _(gp):
        gp.wait_ge(SEM["SC"], sc_done)
        wt(gp, "GW")
        wt(gp, "LD")
        gp.tensor_mul(s_sigD[:], s_sig[:], s_dbc[:])
        inc(gp.drain(), "GP")

    NU = nch * 2
    dzm = {}
    ksp = {}
    k2w = {}
    k2e = {}
    dam = {}
    v0m = {}
    cdm = {}
    p16m = {}
    lpcm = {}
    wm = {}
    qm = {}
    qevm = {}
    ym = {}
    ymulm = {}
    yaddm = {}
    rslc = {}
    rsqc = {}

    def HSL(h):
        return slice(h * 512, (h + 1) * 512)

    def emit_dz(u):
        c, h = u // 2, u % 2
        tt0, tt1 = c * tpc, c * tpc + 1

        @block.tensor
        def _(te, u=u, tt0=tt0, tt1=tt1, h=h):
            if u == 0:
                te.wait_ge(SEM["SC"], sc_done)
                te.wait_ge(SEM["DV"], dv_done)
            else:
                pw(te, ("SC", ksp[u - 1]))
            te.ldweights(s_dp[:, tt0 * 128:(tt0 + 1) * 128])
            te.matmul(p_z0[:], s_dp[:, tt0 * 128:(tt0 + 1) * 128],
                      s_dtw[:, HSL(h)], start=True, stop=True)
            te.ldweights(s_dp[:, tt1 * 128:(tt1 + 1) * 128])
            inc(te.matmul(p_z1[:], s_dp[:, tt1 * 128:(tt1 + 1) * 128],
                          s_dtw[:, HSL(h)], start=True, stop=True), "PE")
        dzm[u] = C["PE"]

    def emit_ew_final(u):
        c, h = u // 2, u % 2
        tt0, tt1 = c * tpc, c * tpc + 1

        @block.scalar
        def _(sc, u=u, h=h, tt0=tt0, tt1=tt1):
            pw(sc, ("PE", cdm[u]))
            pw(sc, ("DV", wm[u - 1]))
            sc.activation(s_we[:, 0, :], p_c0[:], AF.Exp, scale=-1.0)
            sc.drain()
            sc.activation(s_we[:, 1, :], p_c1[:], AF.Exp, scale=-1.0)
            inc(sc.drain(), "SC")
            sc.activation(sE[:, tt0, HSL(h)], p_c0[:], AF.Exp)
            sc.drain()
            sc.activation(sE[:, tt1, HSL(h)], p_c1[:], AF.Exp)
            inc(sc.drain(), "SC")
        k2w[u] = C["SC"] - 1
        k2e[u] = C["SC"]

    def emit_W(u):
        @block.vector
        def _(ve, u=u):
            pw(ve, ("SC", k2w[u]))
            if u >= 1:
                pw(ve, ("PE", ym[u - 1]))
            ve.tensor_mul(s_wc[:, 0, :], s_we[:, 0, :], s_v0[:, 0, :])
            ve.drain()
            ve.tensor_mul(s_wc[:, 1, :], s_we[:, 1, :], s_v0[:, 1, :])
            inc(ve.drain(), "DV")
        wm[u] = C["DV"]

    def emit_qevac_ymul(u):
        c, h = u // 2, u % 2
        tt0, tt1 = c * tpc, c * tpc + 1

        @block.vector
        def _(ve, u=u, h=h, c=c):
            pw(ve, ("PE", qm[u]))
            if h == 0 and c >= 1:
                pw(ve, ("RSQ", rsqc[c - 1]))
            ve.tensor_copy(s_n16q[:, h, :], p_q[:])
            inc(ve.drain(), "DV")
        qevm[u] = C["DV"]

        if h == 1:
            @block.sync
            def _(sync, u=u, c=c):
                pw(sync, ("DV", qevm[u]))
                inc(sync.dma_start(
                    out=s_qpk[:, c, :],
                    in_=s_n16q[:].rearrange("p h (j f) -> p h j f", j=DT // 2)), "RSQ", 16)
            rsqc[c] = C["RSQ"]

        @block.vector
        def _(ve, u=u, h=h, tt0=tt0, tt1=tt1):
            pw(ve, ("PE", ym[u]))
            pw(ve, ("SC", k2e[u]))
            if u >= 2:
                pw(ve, ("GP", yaddm[u - 2]))
            ve.tensor_mul(s_y[:, u % 2, 0, :], sE[:, tt0, HSL(h)], p_y0[:])
            ve.drain()
            ve.tensor_mul(s_y[:, u % 2, 1, :], sE[:, tt1, HSL(h)], p_y1[:])
            inc(ve.drain(), "DV")
        ymulm[u] = C["DV"]

        @block.gpsimd
        def _(gp, u=u, h=h, tt0=tt0, tt1=tt1):
            pw(gp, ("DV", ymulm[u]))
            for i, tt in enumerate((tt0, tt1)):
                gp.tensor_add(sx[:, tt, HSL(h)], sx[:, tt, HSL(h)], s_y[:, u % 2, i, :])
                gp.drain()
            inc(gp.drain(), "GP")
        yaddm[u] = C["GP"]

    def emit_p16_lp_q_y(u):
        c, h = u // 2, u % 2
        tt0, tt1 = c * tpc, c * tpc + 1

        @block.tensor
        def _(te, u=u):
            if u >= 1:
                pw(te, ("DV", lpcm[u - 1]))
            te.ldweights(s_onb[:, 0:NST])
            for i in range(tpc):
                mm = te.matmul(p_lp[:], s_onb[:, 0:NST], s_da[:, u % 2, i, :],
                               start=(i == 0), stop=(i == tpc - 1))
            inc(mm, "PE")
        p16m[u] = C["PE"]

        @block.vector
        def _(ve, u=u, h=h, c=c):
            pw(ve, ("PE", p16m[u]))
            if h == 0 and c >= 1:
                pw(ve, ("RSL", rslc[c - 1]))
            ve.tensor_copy(s_n16l[:, h, :], p_lp[:])
            inc(ve.drain(), "DV")
        lpcm[u] = C["DV"]

        if h == 1:
            @block.sync
            def _(sync, u=u, c=c):
                pw(sync, ("DV", lpcm[u]))
                inc(sync.dma_start(
                    out=s_lppk[:, c, :],
                    in_=s_n16l[:].rearrange("p h (j f) -> p h j f", j=DT // 2)), "RSL", 16)
            rslc[c] = C["RSL"]

        @block.tensor
        def _(te, u=u, tt0=tt0, tt1=tt1):
            pw(te, ("DV", wm[u]))
            if u >= 1:
                pw(te, ("DV", qevm[u - 1]))
            for i, tt in enumerate((tt0, tt1)):
                te.ldweights(s_btm[:, tt, :])
                mm = te.matmul(p_q[:], s_btm[:, tt, :], s_wc[:, i, :],
                               start=(i == 0), stop=(i == tpc - 1))
            inc(mm, "PE")
        qm[u] = C["PE"]

        @block.tensor
        def _(te, u=u, c=c):
            pw(te, ("DV", wm[u]))
            if u >= 1:
                pw(te, ("DV", ymulm[u - 1]))
            te.ldweights(s_gm[:, c, 0:128])
            te.matmul(p_y0[:], s_gm[:, c, 0:128], s_wc[:, 0, :], start=True, stop=True)
            te.ldweights(s_gm[:, c, 128:256])
            te.matmul(p_y1[:], s_gm[:, c, 128:256], s_wc[:, 0, :], start=True, stop=False)
            te.ldweights(s_gm[:, c, 256:384])
            inc(te.matmul(p_y1[:], s_gm[:, c, 256:384], s_wc[:, 1, :],
                          start=False, stop=True), "PE")
        ym[u] = C["PE"]

    emit_dz(0)
    for u in range(NU):
        c, h = u // 2, u % 2
        tt0, tt1 = c * tpc, c * tpc + 1

        @block.scalar
        def _(sc, u=u, h=h):
            sc.activation(s_dum[:], s_abc[0:1, 0:1], AF.Exp)
            sc.drain()
            pw(sc, ("PE", dzm[u]))
            if u >= 2:
                pw(sc, ("DV", v0m[u - 2]))
            sc.activation(s_delta[:, 0, HSL(h)], p_z0[:], AF.Exp)
            sc.drain()
            sc.activation(s_delta[:, 0, HSL(h)], s_delta[:, 0, HSL(h)], AF.Ln, bias=1.0)
            sc.drain()
            sc.activation(s_delta[:, 1, HSL(h)], p_z1[:], AF.Exp)
            sc.drain()
            sc.activation(s_delta[:, 1, HSL(h)], s_delta[:, 1, HSL(h)], AF.Ln, bias=1.0)
            inc(sc.drain(), "SC")
            if u >= 1:
                up, hp = u - 1, (u - 1) % 2
                tp0, tp1 = (up // 2) * tpc, (up // 2) * tpc + 1
                pw(sc, ("PE", cdm[up]))
                if up >= 1:
                    pw(sc, ("DV", wm[up - 1]))
                sc.activation(s_we[:, 0, :], p_c0[:], AF.Exp, scale=-1.0)
                sc.drain()
                sc.activation(s_we[:, 1, :], p_c1[:], AF.Exp, scale=-1.0)
                inc(sc.drain(), "SC")
                sc.activation(sE[:, tp0, HSL(hp)], p_c0[:], AF.Exp)
                sc.drain()
                sc.activation(sE[:, tp1, HSL(hp)], p_c1[:], AF.Exp)
                inc(sc.drain(), "SC")
        ksp[u] = C["SC"] - (2 if u >= 1 else 0)
        if u >= 1:
            k2w[u - 1] = C["SC"] - 1
            k2e[u - 1] = C["SC"]

        @block.vector
        def _(ve, u=u, h=h):
            pw(ve, ("SC", ksp[u]))
            if u >= 2:
                pw(ve, ("PE", p16m[u - 2]))
            ve.tensor_mul(s_da[:, u % 2, 0, :], s_delta[:, 0, HSL(h)], s_abc[:, HSL(h)])
            ve.drain()
            ve.tensor_mul(s_da[:, u % 2, 1, :], s_delta[:, 1, HSL(h)], s_abc[:, HSL(h)])
            inc(ve.drain(), "DV")
        dam[u] = C["DV"]

        if u >= 1:
            emit_W(u - 1)

        @block.vector
        def _(ve, u=u, h=h, tt0=tt0, tt1=tt1):
            ve.tensor_mul(s_v0[:, 0, :], s_delta[:, 0, HSL(h)], s_u[:, tt0, HSL(h)])
            ve.drain()
            ve.tensor_mul(s_v0[:, 0, :], s_v0[:, 0, :], s_sig[:, HSL(h)])
            ve.drain()
            ve.tensor_mul(s_v0[:, 1, :], s_delta[:, 1, HSL(h)], s_u[:, tt1, HSL(h)])
            ve.drain()
            ve.tensor_mul(s_v0[:, 1, :], s_v0[:, 1, :], s_sig[:, HSL(h)])
            inc(ve.drain(), "DV")
        v0m[u] = C["DV"]

        @block.tensor
        def _(te, u=u):
            pw(te, ("DV", dam[u]))
            if u >= 1:
                pw(te, ("SC", k2e[u - 1]))
            te.ldweights(s_utb[:])
            te.matmul(p_c0[:], s_utb[:], s_da[:, u % 2, 0, :], start=True, stop=True)
            te.ldweights(s_onb[:])
            te.matmul(p_c1[:], s_onb[:], s_da[:, u % 2, 0, :], start=True, stop=False)
            te.ldweights(s_utb[:])
            inc(te.matmul(p_c1[:], s_utb[:], s_da[:, u % 2, 1, :],
                          start=False, stop=True), "PE")
        cdm[u] = C["PE"]

        if u + 1 < NU:
            emit_dz(u + 1)
        if u >= 2:
            emit_qevac_ymul(u - 2)
        if u >= 1:
            emit_p16_lp_q_y(u - 1)

    emit_ew_final(NU - 1)
    emit_W(NU - 1)
    emit_qevac_ymul(NU - 2)
    emit_p16_lp_q_y(NU - 1)
    emit_qevac_ymul(NU - 1)
    esP1.close()

    esF1 = ExitStack()
    s_send = esF1.enter_context(nc.sbuf_tensor("s_send", [128, 2 * BATCH * 128], F32))
    s_pall = esF1.enter_context(nc.sbuf_tensor("s_pall", [128, nch, 128], F32))
    s_fh = esF1.enter_context(nc.sbuf_tensor("s_fh", [128, BATCH, 128], F32))
    s_ft = esF1.enter_context(nc.sbuf_tensor("s_ft", [128, BATCH, 128], F32))
    s_fp = esF1.enter_context(nc.sbuf_tensor("s_fp", [128, BATCH, 128], F32))

    @block.scalar
    def _(sc):
        wt(sc, "RSL")
        wt(sc, "RSQ")
        wt(sc, "DV")
        wt(sc, "GP")
        sc.activation(s_pall[:], s_lppk[:], AF.Exp)
        inc(sc.drain(), "SC")

    @block.vector
    def _(ve):
        wt(ve, "SC")
        ve.tensor_mul(s_fh[:], s_pall[:, 0::ncpb, :], s_qpk[:, 0::ncpb, :])
        ve.drain()
        for lc in range(1, ncpb):
            ve.tensor_add(s_ft[:], s_fh[:], s_qpk[:, lc::ncpb, :])
            ve.drain()
            ve.tensor_mul(s_fh[:], s_ft[:], s_pall[:, lc::ncpb, :])
            ve.drain()
        inc(ve.drain(), "DV")

    @block.vector
    def _(ve):
        for b in range(BATCH):
            ve.tensor_copy(s_send[:, (2 * b) * 128:(2 * b + 1) * 128], s_fh[:, b, :])
        ve.drain()
        ve.tensor_copy(s_ft[:], s_lppk[:, 0::ncpb, :])
        ve.drain()
        for lc in range(1, ncpb):
            ve.tensor_add(s_ft[:], s_ft[:], s_lppk[:, lc::ncpb, :])
            ve.drain()
        for b in range(BATCH):
            ve.tensor_copy(s_send[:, (2 * b + 1) * 128:(2 * b + 2) * 128], s_ft[:, b, :])
        inc(ve.drain(), "DV")

    @block.gpsimd
    def _(gp):
        wt(gp, "DV")
        inc(gp.dma_start(out=cc_in.ap(), in_=s_send[:]), "GW", 16)
        wt(gp, "GW")
        inc(gp.collective_compute(
            "AllGather", ALU.bypass,
            replica_groups=[list(range(NCORES))],
            ins=[cc_in.ap().opt()],
            outs=[cc_out.ap().opt()]), "CC")
    esF1.close()

    ud_mark = {}

    def emit_ud(c):
        @block.gpsimd
        def _(gp, c=c):
            pw(gp, ("GP", yaddm[2 * c + 1]))
            for i in range(tpc):
                tt = c * tpc + i
                gp.tensor_mul(s_gpscr[:], s_u[:, tt, :], s_sigD[:])
                gp.drain()
                gp.tensor_add(sx[:, tt, :], sx[:, tt, :], s_gpscr[:])
                gp.drain()
            inc(gp.drain(), "GP")
        ud_mark[c] = C["GP"]

    emit_ud(0)
    emit_ud(1)

    esF2 = ExitStack()
    s_ag = esF2.enter_context(nc.sbuf_tensor("s_ag", [128, NCORES, 2 * BATCH * 128], F32))
    s_hin = esF2.enter_context(nc.sbuf_tensor("s_hin", [128, BATCH, 128], F32))
    s_h0pk = esF2.enter_context(nc.sbuf_tensor("s_h0pk", [128, nch, 128], F32))
    s_fh2 = esF2.enter_context(nc.sbuf_tensor("s_fh2", [128, BATCH, 128], F32))
    s_ft2 = esF2.enter_context(nc.sbuf_tensor("s_ft2", [128, BATCH, 128], F32))
    s_fp2 = esF2.enter_context(nc.sbuf_tensor("s_fp2", [128, BATCH, 128], F32))
    esF2b = ExitStack()
    s_pall2 = esF2b.enter_context(nc.sbuf_tensor("s_pall2", [128, nch, 128], F32))

    @block.gpsimd
    def _(gp):
        wt(gp, "CC")
        inc(gp.dma_start(out=s_ag[:],
                         in_=cc_out.ap().rearrange("(j p) f -> p j f", p=128)), "GW", 16)

    emit_ud(2)
    emit_ud(3)

    @block.vector
    def _(ve):
        wt(ve, "GW")
        ve.memset(s_hin[:], 0.0)
        inc(ve.drain(), "DV")

    s_ag4 = s_ag[:, :, :].rearrange("p j (k f) -> p j k f", k=2 * BATCH)

    @block.scalar
    def _(sc):
        wt(sc, "DV")
        for j in range(NCORES):
            sc.activation(s_ag4[:, j, 1::2, :], s_ag4[:, j, 1::2, :], AF.Exp,
                          scale=s_mask[:, j:j + 1])
        inc(sc.drain(), "SC")

    @block.vector
    def _(ve):
        wt(ve, "SC")
        for j in range(NCORES):
            ve.tensor_mul(s_ft2[:], s_hin[:], s_ag4[:, j, 1::2, :])
            ve.drain()
            ve.scalar_tensor_tensor(
                s_hin[:], s_ag4[:, j, 0::2, :],
                s_mask[:, j:j + 1], s_ft2[:], ALU.mult, ALU.add)
            ve.drain()
        inc(ve.drain(), "DV")

    @block.scalar
    def _(sc):
        wt(sc, "DV")
        sc.activation(s_pall2[:], s_lppk[:], AF.Exp)
        inc(sc.drain(), "SC")

    @block.vector
    def _(ve):
        wt(ve, "SC")
        ve.tensor_copy(s_fh2[:], s_hin[:])
        ve.drain()
        for lc in range(ncpb):
            ve.tensor_copy(s_h0pk[:, lc::ncpb, :], s_fh2[:])
            ve.drain()
            if lc < ncpb - 1:
                ve.tensor_add(s_ft2[:], s_fh2[:], s_qpk[:, lc::ncpb, :])
                ve.drain()
                ve.tensor_mul(s_fh2[:], s_ft2[:], s_pall2[:, lc::ncpb, :])
                ve.drain()
        inc(ve.drain(), "DV")

    esF2b.close()
    esP2 = ExitStack()
    s_h0bf = esP2.enter_context(nc.sbuf_tensor("s_h0bf", [NST, 3, D], BF16))
    s_scr2 = esP2.enter_context(nc.sbuf_tensor("s_scr2", [128, D], F32))
    p_w0 = esP2.enter_context(nc.psum_tensor("p_w0", [128, D], F32))
    p_w1 = esP2.enter_context(nc.psum_tensor("p_w1", [128, D], F32))

    dv_fold = C["DV"]
    pe_h0use = {}
    rsh_names = ("RSH", "RSH2", "RSH3")
    rsh_marks = {}

    def emit_h0(c):
        nm = rsh_names[c % 3]

        @block.gpsimd
        def _(gp, c=c, nm=nm, dv_fold=dv_fold):
            gp.wait_ge(SEM["DV"], dv_fold)
            if c >= 3:
                pw(gp, ("PE", pe_h0use[c - 3]))
            inc(gp.dma_start(out=s_h0bf[:, c % 3, :].rearrange("p (g f) -> p g f", g=8),
                             in_=s_h0pk[:, c, :]), nm, 16)
        rsh_marks[c] = (nm, C[nm])

    for c in range(3):
        emit_h0(c)

    for c in range(nch):
        t0 = c * tpc
        for i in range(tpc):
            tt = t0 + i
            py = p_w0 if i == 0 else p_w1

            @block.tensor
            def _(te, c=c, tt=tt, py=py):
                pw(te, rsh_marks[c])
                wt(te, "DV")
                te.matmul(py[:, 0:512], sC[:, tt * 128:(tt + 1) * 128],
                          s_h0bf[:, c % 3, 0:512], start=True, stop=True)
                inc(te.matmul(py[:, 512:1024], sC[:, tt * 128:(tt + 1) * 128],
                              s_h0bf[:, c % 3, 512:1024], start=True, stop=True), "PE")
        pe_h0use[c] = C["PE"]

        for i in range(tpc):
            tt = t0 + i
            py = p_w0 if i == 0 else p_w1

            @block.vector
            def _(ve, tt=tt, py=py, c=c, i=i):
                wt(ve, "PE")
                if i == 0:
                    pw(ve, ("GP", ud_mark[c]))
                ve.tensor_mul(s_scr2[:], sE[:, tt, :], py[:])
                ve.drain()
                ve.tensor_add(sx[:, tt, :], sx[:, tt, :], s_scr2[:])
                inc(ve.drain(), "DV")

        if c + 3 < nch:
            if c + 4 < nch:
                emit_ud(c + 4)
            emit_h0(c + 3)
    esP2.close()
    esF2.close()
    esDB.close()
    esU.close()
    esM.close()

    esFN = ExitStack()
    s_x2t = esFN.enter_context(nc.sbuf_tensor("s_x2t", [128, DT, ntok], BF16))
    s_hq = esFN.enter_context(nc.sbuf_tensor("s_hq", [128, HT, ntok], FP8))
    s_bq = esFN.enter_context(nc.sbuf_tensor("s_bq", [128, D], F16))
    s_w2q = esFN.enter_context(nc.sbuf_tensor("s_w2q", [128, HT, D], FP8))

    @block.gpsimd
    def _(gp):
        wt(gp, "DV")
        inc(gp.dma_start(out=s_bq[:], in_=b2bc), "GW", 16)
        for h in range(HT):
            inc(gp.dma_start(out=s_w2q[:, h, :], in_=w2t[h, :, :]), "GW", 16)


    @block.vector
    def _(ve):
        wt(ve, "GP")
        inc(ve.drain(), "DV")

    w1cnt = {"a": 0, "b": 0}
    ntb_h = htok // 512
    w1cnt = {"a": 0, "b": 0}
    ntb_h = htok // 512

    ptb_free = [(None, 0), (None, 0)]

    def emit_T_grp(tbg, sc_ln, p_tbA, p_tbB):
        for g in range(DT):
            par = g % 2
            ptr = p_tbA if par == 0 else p_tbB

            @block.tensor
            def _(te, g=g, sc_ln=sc_ln, ptbf=ptb_free[par], ptr=ptr):
                te.wait_ge(SEM["SC"], sc_ln)
                pw(te, ptbf)
                for q in range(4):
                    inc(te.transpose(ptr[:, q * 128:(q + 1) * 128],
                                     s_x2[:, q, g * 128:(g + 1) * 128],
                                     s_idb[:]), "PE")
            pe_t = C["PE"]

            @block.vector
            def _(ve, tbg=tbg, g=g, pe_t=pe_t, ptr=ptr):
                ve.wait_ge(SEM["PE"], pe_t)
                ve.tensor_copy(s_x2t[:, g, tbg * 512:(tbg + 1) * 512], ptr[:])
                inc(ve.drain(), "DV")
            ptb_free[par] = ("DV", C["DV"])
        return C["PE"]

    def emit_m1_all():
        esM1 = ExitStack()
        s_w1a = esM1.enter_context(nc.sbuf_tensor("s_w1a", [128, DT, 128], BF16))
        s_w1b = esM1.enter_context(nc.sbuf_tensor("s_w1b", [128, DT, 128], BF16))
        psums = [esM1.enter_context(nc.psum_tensor("p_m1_%d" % i, [128, ntok], F32))
                 for i in range(2)]
        evac_m = {}

        @block.gpsimd
        def _(gp):
            wt(gp, "DV")
            wt(gp, "PE")
            inc(gp.dma_start(out=s_w1a[:], in_=w1t[0, :, :, :]), "W1a", 16)
            w1cnt["a"] += 1

        @block.gpsimd
        def _(gp):
            inc(gp.dma_start(out=s_w1b[:], in_=w1t[1, :, :, :]), "W1b", 16)
            w1cnt["b"] += 1
        gpw1_done = C["W1b"]
        dv_x2t = C["DV"]
        ntb = ntok // 512
        pe_marks = {}
        for h in range(HT):
            par = "a" if h % 2 == 0 else "b"
            pm = psums[h % 2]
            slab = s_w1a if h % 2 == 0 else s_w1b

            @block.tensor
            def _(te, h=h, pm=pm, slab=slab, par=par, cnt=w1cnt[par], dv_x2t=dv_x2t):
                te.wait_ge(SEM["DV"], dv_x2t)
                te.wait_ge(SEM["W1" + par], 16 * cnt)
                if h >= 2:
                    pw(te, ("SC", evac_m[h - 2]))
                for g in range(DT):
                    te.ldweights(slab[:, g, :])
                    for tb in range(ntb):
                        mm = te.matmul(pm[:, tb * 512:(tb + 1) * 512], slab[:, g, :],
                                       s_x2t[:, g, tb * 512:(tb + 1) * 512],
                                       start=(g == 0), stop=(g == DT - 1))
                        if g == DT - 1 and tb == ntb - 1:
                            inc(mm, "PE")
            pe_marks[h] = C["PE"]

            if h + 2 < HT:
                if h % 2 == 0:
                    @block.gpsimd
                    def _(gp, h=h, pe_h=pe_marks[h]):
                        gp.wait_ge(SEM["PE"], pe_h)
                        inc(gp.dma_start(out=s_w1a[:], in_=w1t[h + 2, :, :, :]), "W1a", 16)
                    w1cnt["a"] += 1
                else:
                    @block.gpsimd
                    def _(gp, h=h, pe_h=pe_marks[h]):
                        gp.wait_ge(SEM["PE"], pe_h)
                        inc(gp.dma_start(out=s_w1b[:], in_=w1t[h + 2, :, :, :]), "W1b", 16)
                    w1cnt["b"] += 1

            @block.scalar
            def _(sc, h=h, pm=pm, pe_h=pe_marks[h]):
                sc.wait_ge(SEM["PE"], pe_h)
                for tb in range(ntb):
                    sc.activation(s_hq[:, h, tb * 512:(tb + 1) * 512],
                                  pm[:, tb * 512:(tb + 1) * 512], AF.Relu,
                                  bias=s_b1[:, h:h + 1])
                inc(sc.drain(), "SC")
            evac_m[h] = C["SC"]
        esM1.close()
        return C["SC"]

    def emit_m2(half, m1_sc):
        tbase = half * hnt
        esM2 = ExitStack()
        s_tf16 = esM2.enter_context(nc.sbuf_tensor("s_tf16_%d" % half, [128, D], F16))
        pms = [esM2.enter_context(nc.psum_tensor("p_m2_%d_%d" % (half, i), [128, D], F32))
               for i in range(2)]
        gw_w2 = C["GW"]
        m2_evac = [C["DV"], C["DV"]]
        for t in range(hnt):
            pm = pms[t % 2]

            @block.tensor
            def _(te, t=t, pm=pm, ev=m2_evac[t % 2], gw_w2=gw_w2, m1_sc=m1_sc, tbase=tbase):
                te.wait_ge(SEM["SC"], m1_sc)
                te.wait_ge(SEM["GW"], gw_w2)
                te.wait_ge(SEM["DV"], ev)
                DR = mybir.MatmulPerfMode.DoubleRow
                for hp in range(0, HT, 2):
                    lhs = s_hq[:, hp:hp + 2, (tbase + t) * 128:(tbase + t + 1) * 128]
                    te.ldweights(lhs, perf_mode=DR)
                    mm = te.matmul(pm[:, 0:512], lhs, s_w2q[:, hp:hp + 2, 0:512],
                                   start=(hp == 0), stop=(hp == HT - 2), perf_mode=DR)
                    mm = te.matmul(pm[:, 512:1024], lhs, s_w2q[:, hp:hp + 2, 512:1024],
                                   start=(hp == 0), stop=(hp == HT - 2), perf_mode=DR)
                    if hp == HT - 2:
                        inc(mm, "PE")
            pe_t = C["PE"]

            @block.vector
            def _(ve, t=t, pm=pm, pe_t=pe_t, tbase=tbase):
                ve.wait_ge(SEM["PE"], pe_t)
                ve.tensor_add(s_tf16[:], pm[:], s_bq[:])
                ve.drain()
                ve.tensor_add(sx[:, tbase + t, :], sx[:, tbase + t, :], s_tf16[:])
                inc(ve.drain(), "DV")
            m2_evac[t % 2] = C["DV"]
        esM2.close()

    esX2 = ExitStack()
    s_x2 = esX2.enter_context(nc.sbuf_tensor("s_x2", [128, 4, D], BF16))
    esT0 = ExitStack()
    p_tbA0 = esT0.enter_context(nc.psum_tensor("p_tbA0", [128, 512], BF16))
    p_tbB0 = esT0.enter_context(nc.psum_tensor("p_tbB0", [128, 512], BF16))
    ngrpT = nt // 4
    sc_ffn_base = C["SC"]
    prev_pe = 0
    for tbg in range(ngrpT):
        sc_ln = emit_ln_range([tbg * 4 + q for q in range(4)],
                              lambda k, t: s_x2[:, k, :],
                              first_wait=(None if tbg == 0 else ("SC", sc_ffn_base)),
                              norm_wait=prev_pe, idx0=tbg * 4)
        prev_pe = emit_T_grp(tbg, sc_ln, p_tbA0, p_tbB0)
    esT0.close()
    esX2.close()

    m1_sc = emit_m1_all()
    emit_m2(0, m1_sc)
    emit_m2(1, m1_sc)
    esFN.close()

    @block.sync
    def _(sync):
        wt(sync, "DV")
        wt(sync, "GP")
        for t in range(nt):
            inc(sync.dma_start(out=y_out[t * 128:(t + 1) * 128, :], in_=sx[:, t, :]), "ST", 16)
        sync.wait_ge(SEM["ST"], C["ST"])

    es.close()
    return nc


def _host_prep(L0_attn_x, L0_attn_A_log, L0_attn_x_proj_w, L0_attn_dt_proj_w,
               L0_attn_dt_proj_b, L0_attn_D_param, L0_attn_trend_emphasis,
               L0_ffn_w1, L0_ffn_b1, L0_ffn_w2, L0_ffn_b2, lcore):
    f = np.ascontiguousarray
    x = np.asarray(L0_attn_x, dtype=np.float32)
    w1 = np.asarray(L0_ffn_w1, dtype=np.float32)
    w2 = np.asarray(L0_ffn_w2, dtype=np.float32)
    xpw = np.asarray(L0_attn_x_proj_w, dtype=np.float32)
    shared = {
        "w1t": f(w1.T.reshape(DT, 128, HT, 128).transpose(2, 1, 0, 3)),
        "w2t": f(w2.T.reshape(HT, 128, D)),
        "b1t": f(np.asarray(L0_ffn_b1, np.float32).reshape(HT, 128).T),
        "b2bc": f(np.broadcast_to(np.asarray(L0_ffn_b2, np.float32), (128, D))),
        "tebc": f(np.broadcast_to(np.asarray(L0_attn_trend_emphasis, np.float32), (128, D))),
        "dbc": f(np.broadcast_to(np.asarray(L0_attn_D_param, np.float32), (128, D))),
        "alogT": f(np.asarray(L0_attn_A_log, np.float32).T),
        "xpwd": f(xpw[:RANK].T.reshape(DT, 128, RANK).transpose(1, 0, 2)),
        "xpwb": f(xpw[RANK:RANK + NST].T.reshape(DT, 128, NST).transpose(1, 0, 2)),
        "xpwc": f(xpw[RANK + NST:].T.reshape(DT, 128, NST).transpose(1, 0, 2)),
        "dtwtb": f(np.concatenate([np.asarray(L0_attn_dt_proj_w, np.float32).T,
                                   np.asarray(L0_attn_dt_proj_b, np.float32)[None, :]], axis=0)),
        "identf": np.eye(128, dtype=np.float32),
        "tet": f(np.asarray(L0_attn_trend_emphasis, np.float32).reshape(DT, 128).T),
        "utf": f(np.triu(np.ones((128, 128), dtype=np.float32))),
    }
    in_maps = []
    for k in range(NCORES):
        m = dict(shared)
        m["x_in"] = f(x[:, k * lcore:(k + 1) * lcore, :].reshape(BATCH * lcore, D))
        mask = np.zeros((128, NCORES), dtype=np.float32)
        mask[:, :k] = 1.0
        m["maskin"] = mask
        in_maps.append(m)
    return in_maps


def kernel(**inputs):
    lcore = np.asarray(inputs["L0_attn_x"]).shape[1] // NCORES
    in_maps = _host_prep(lcore=lcore, **inputs)
    nc = build(lcore=lcore)
    res = run_bass_kernel_spmd(nc, in_maps, core_ids=list(range(NCORES)))
    out = np.empty((BATCH, NCORES * lcore, D), dtype=np.float32)
    for k in range(NCORES):
        y = np.asarray(res.results[k]["y_out"]).reshape(BATCH, lcore, D)
        out[:, k * lcore:(k + 1) * lcore, :] = y
    return out
